# version 27
# baseline (speedup 1.0000x reference)
import numpy as np
import ml_dtypes

import concourse.bass as bass
import concourse.mybir as mybir
from concourse.bass_utils import run_bass_kernel_spmd
from concourse.bass import IndirectOffsetOnAxis

F32 = mybir.dt.float32
I32 = mybir.dt.int32
BF16 = mybir.dt.bfloat16
AF = mybir.ActivationFunctionType
ALU = mybir.AluOpType
AX = mybir.AxisListType.X
BF = ml_dtypes.bfloat16

H = 16
DH = 128
T = 2048
DM = 2048
NQ = 1024
NCH = 16
SB = 204
RB = 819
EVICT = T - RB
NK = SB + RB
LAST_LOC = 768
II = 1408
ICN = 11
NVE = 18
SLAB = ICN * 2048

U_TABLE = np.array([
    [0.6102175712585449, 0.6966079473495483, 0.06568920612335205, 0.6029453277587891, 0.4143662452697754, 0.24911487102508545, 0.5782934427261353, 0.24924850463867188, 0.959638237953186, 0.8191142082214355, 0.3645491600036621, 0.059471845626831055, 0.9161832332611084, 0.6182148456573486, 0.8282822370529175, 0.6976723670959473],
    [0.756269097328186, 0.3594318628311157, 0.4266716241836548, 0.477594256401062, 0.08340179920196533, 0.40768611431121826, 0.6278458833694458, 0.3251793384552002, 0.35155820846557617, 0.15171575546264648, 0.5589374303817749, 0.8866020441055298, 0.3201930522918701, 0.12432277202606201, 0.6967982053756714, 0.8069779872894287],
    [0.6127028465270996, 0.7509020566940308, 0.4151843786239624, 0.4897202253341675, 0.11665701866149902, 0.40220367908477783, 0.08376991748809814, 0.9528622627258301, 0.9559504985809326, 0.06945085525512695, 0.20124602317810059, 0.06397080421447754, 0.8091422319412231, 0.31807422637939453, 0.639366865158081, 0.8050472736358643],
    [0.5085939168930054, 0.5088651180267334, 0.9745233058929443, 0.3331362009048462, 0.05924034118652344, 0.7208822965621948, 0.3291349411010742, 0.5319250822067261, 0.4839836359024048, 0.643956184387207, 0.19713079929351807, 0.1282445192337036, 0.3164334297180176, 0.4948718547821045, 0.9159883260726929, 0.781721830368042],
], dtype=np.float32)

KEPT = np.concatenate([np.arange(0, SB), np.arange(EVICT, T)])

ARENA_BYTES = 212800



def _attnT_blocks(W):
    return np.ascontiguousarray(
        W.reshape(16, 128, 16, 128).transpose(0, 3, 2, 1).reshape(16, 128, 2048))


def _w13_ic_blocks(w):
    return np.ascontiguousarray(
        np.asarray(w).reshape(11, 128, 16, 128).transpose(3, 0, 2, 1).reshape(128, 11 * 2048))


def _w2_slab(w):
    return np.ascontiguousarray(
        np.asarray(w).reshape(2048, 11, 128).transpose(2, 1, 0).reshape(128, SLAB))


def _rope_tables(pos, scale):
    inv = 1.0 / (10000.0 ** (np.arange(0, DH, 2, dtype=np.float64) / DH))
    invf = np.concatenate([inv, inv])
    ang = pos[:, None].astype(np.float64) * invf[None, :]
    cos = (np.cos(ang) * scale).astype(np.float32)
    sin = (np.sin(ang) * scale).astype(np.float32)
    sin[:, :64] *= -1.0
    return (np.ascontiguousarray(cos.T).astype(BF), np.ascontiguousarray(sin.T).astype(BF))



class TV:
    def __init__(self, handle, elem_off, cols, p0=0, np_=128):
        self.h = handle
        self.e0 = elem_off
        self.cols = cols
        self.p0 = p0
        self.np_ = np_

    def __getitem__(self, idx):
        ps, fs = idx
        pstart = self.p0 + (ps.start or 0)
        pstop = self.p0 + (ps.stop if ps.stop is not None else self.np_)
        fstart = self.e0 + (fs.start or 0)
        fstop = self.e0 + (fs.stop if fs.stop is not None else self.cols)
        return self.h[pstart:pstop, fstart:fstop]


class Carver:
    def __init__(self, h32, h16, hi32=None):
        self.h32 = h32
        self.h16 = h16
        self.hi32 = hi32
        self.cursor = 0

    def at(self, byte_off):
        self.cursor = byte_off
        return self

    def f32(self, cols, p0=0, np_=128):
        off = (self.cursor + 511) // 512 * 512
        self.cursor = off + cols * 4
        assert self.cursor <= ARENA_BYTES, f"arena overflow at {self.cursor}"
        return TV(self.h32, off // 4, cols, p0, np_)

    def b16(self, cols, p0=0, np_=128):
        off = (self.cursor + 511) // 512 * 512
        self.cursor = off + cols * 2
        assert self.cursor <= ARENA_BYTES, f"arena overflow at {self.cursor}"
        return TV(self.h16, off // 2, cols, p0, np_)

    def i32(self, cols, p0=0, np_=128):
        off = (self.cursor + 511) // 512 * 512
        self.cursor = off + cols * 4
        assert self.cursor <= ARENA_BYTES, f"arena overflow at {self.cursor}"
        return TV(self.hi32, off // 4, cols, p0, np_)


class Prog:
    ENGS = ("tensor", "vector", "scalar", "gpsimd", "sync")

    def __init__(self, nc, sems):
        self.nc = nc
        self.q = {e: [] for e in self.ENGS}
        self.sems = sems
        self.ct = {k: 0 for k in sems}

    def wait(self, eng, sem, val=None):
        v = self.ct[sem] if val is None else val
        if v is None or v <= 0:
            return
        s = self.sems[sem]
        self.q[eng].append(lambda E: E.wait_ge(s, v))

    SELF_SEM = {"vector": "dv", "scalar": "ac", "gpsimd": "gp"}

    def op(self, eng, build, inc=None, k=1):
        ss = self.SELF_SEM.get(eng)
        if ss is not None:
            prev = self.ct[ss]
            if prev > 0:
                s0 = self.sems[ss]
                self.q[eng].append(lambda E, s0=s0, prev=prev: E.wait_ge(s0, prev))
            if inc is None:
                inc = ss
            assert inc == ss, (eng, inc)
        if inc is None:
            self.q[eng].append(build)
            return None
        s = self.sems[inc]
        self.ct[inc] += k
        self.q[eng].append(lambda E: build(E).then_inc(s, k))
        return self.ct[inc]

    def dma(self, eng, out, in_, sem=None):
        if sem is None:
            sem = "s" if eng == "sync" else "g"
        s = self.sems[sem]
        self.ct[sem] += 16
        self.q[eng].append(lambda E: E.dma_start(out=out, in_=in_).then_inc(s, 16))
        return self.ct[sem]

    def iop(self, eng, build, sem):
        s = self.sems[sem]
        self.ct[sem] += 16
        self.q[eng].append(lambda E: build(E).then_inc(s, 16))
        return self.ct[sem]


def build_nc(nh=H, nv=NVE, do_w=True, do_l2=True, do_entry=True):
    nc = bass.Bass("TRN2", target_bir_lowering=False)
    d = {}
    def di(name, shape, dtype):
        d[name] = nc.dram_tensor(name, shape, dtype, kind="ExternalInput").ap()
        return d[name]

    xqT = di("xqT", [DM, NQ], F32)
    xkT = di("xkT", [DM, NQ], F32)
    wq = di("wq", [16, 128, 2048], F32)
    wk = di("wk", [16, 128, 2048], F32)
    wv = di("wv", [16, 128, 2048], F32)
    wo = di("wo", [16, 128, 2048], F32)
    gateT = di("gateT", [128, 256], F32)
    w1v = di("w1v", [NVE, 128, ICN * 2048], BF16)
    w2v = di("w2v", [NVE, 128, SLAB], BF16)
    w3v = di("w3v", [NVE, 128, ICN * 2048], BF16)
    cosq = di("cosq", [128, NQ], BF16)
    sinq = di("sinq", [128, NQ], BF16)
    cosk = di("cosk", [128, NQ], BF16)
    sink = di("sink", [128, NQ], BF16)
    identf = di("identf", [128, 128], F32)
    identb = di("identb", [128, 128], BF16)
    onesrow = di("onesrow", [1, 128], F32)
    onesrowb = di("onesrowb", [1, 128], BF16)
    onescol = di("onescol", [128, 1], F32)
    selmid = di("selmid", [128, 8], BF16)
    sel204 = di("sel204", [128, 1], BF16)
    urow = di("urow", [1, 16], F32)
    epscol = di("epscol", [128, 1], F32)
    zrows = di("zrows", [128, DM + 16], BF16)
    e4col = di("e4col", [16, 1], F32)
    out = nc.dram_tensor("out", [DM, NQ], F32, kind="ExternalOutput").ap()
    disp2 = nc.dram_tensor("disp2", [4096, DM + 16], BF16, kind="Internal").ap()
    yout = nc.dram_tensor("yout", [4096, DM], BF16, kind="Internal").ap()

    arena_guard = nc.sbuf_tensor("AR", [128, ARENA_BYTES // 4], F32)
    arena = arena_guard.__enter__()
    h32 = arena
    h16 = arena.bitcast(BF16)
    hi32 = arena.bitcast(I32)

    SEMNAMES = ("g", "s", "pe", "dv", "ac", "gp", "gc1", "gc2", "gc3",
                "stg0", "stg1", "stg2", "stg3", "wp0", "wp1", "gs", "w2s",
                "ga0", "ga1", "sc", "hx", "yl",
                "wb0", "wb1", "wb2", "wb3", "st0", "st1", "mg0", "mg1")
    sem_guards = [nc.semaphore(n) for n in SEMNAMES]
    sems = {}
    for n, sg in zip(SEMNAMES, sem_guards):
        sems[n] = sg.__enter__()
    p = Prog(nc, sems)

    pA_g = nc.psum_tensor("pA", [128, 1024], F32); pA = pA_g.__enter__()
    pB_g = nc.psum_tensor("pB", [128, 1024], F32); pB = pB_g.__enter__()
    pC_g = nc.psum_tensor("pC", [128, 1024], F32); pC = pC_g.__enter__()
    pT1_g = nc.psum_tensor("pT1", [128, 1024], BF16); pT1 = pT1_g.__enter__()
    pT2_g = nc.psum_tensor("pT2", [128, 1024], BF16); pT2 = pT2_g.__enter__()

    C = Carver(h32, h16, hi32)

    C.at(0)
    s_idf = C.f32(128)
    s_idb = C.b16(128)
    s_ones_r = C.f32(128, p0=0, np_=1)
    s_ones_rb = C.b16(128, p0=0, np_=1)
    s_ones_c = C.f32(1)
    s_eps = C.f32(1)
    GLOB_END = (C.cursor + 511) // 512 * 512

    for dst, src in ((s_idf, identf), (s_idb, identb), (s_ones_r, onesrow), (s_ones_rb, onesrowb), (s_ones_c, onescol), (s_eps, epscol)):
        p.dma("gpsimd", dst[:, :], src[:, :], sem="gc1")
    const_g = p.ct["gc1"]

    C.at(GLOB_END)
    hsqT = C.b16(16 * 1024)
    hskT = C.b16(16 * 1024)
    A_SCRATCH = C.cursor
    xst = [C.f32(1024) for _ in range(2)]
    sqst = [C.f32(1024) for _ in range(2)]
    murowE = C.f32(1024, p0=0, np_=1)
    mubE = C.f32(1024)
    rstdbE = C.f32(1024)
    cenbE = C.f32(1024)

    C.at(A_SCRATCH)
    probsT = C.b16(8 * 1024)
    aoT = C.b16(16 * 1024)
    wqs = [C.b16(2048) for _ in range(2)]
    wks = [C.b16(2048) for _ in range(2)]
    wvs = [C.b16(2048) for _ in range(2)]
    qTt = [C.b16(1024) for _ in range(2)]
    kTt = [C.b16(1024) for _ in range(2)]
    vft = [C.b16(1024) for _ in range(2)]
    vtok = [C.b16(1024) for _ in range(2)]
    rtmp = C.b16(512)
    s_cosq = C.b16(NQ)
    s_sinq = C.b16(NQ)
    s_cosk = C.b16(NQ)
    s_sink = C.b16(NQ)
    junk = C.b16(1024)
    s_selm = C.b16(8)
    s_sel204 = C.b16(1)
    s_urow = C.f32(16, p0=0, np_=1)
    stq = C.f32(8)
    stqT = C.f32(128, p0=0, np_=8)
    negrow = C.f32(1024, p0=0, np_=1)
    negrh = C.b16(1024, p0=0, np_=1)
    negrl = C.b16(1024, p0=0, np_=1)
    sst = C.f32(8)
    sstM = C.f32(8)
    sstL = C.f32(8)
    evs = C.f32(8, p0=0, np_=1)
    vero = C.f32(128, p0=0, np_=1)
    srowb = C.f32(256, p0=0, np_=1)
    cbs = C.b16(1024)
    ATTN_END = C.cursor
    assert ATTN_END <= ARENA_BYTES
    xst2 = [C.f32(1024) for _ in range(2)]
    sqst2 = [C.f32(1024) for _ in range(2)]
    p1ev = [C.f32(1024) for _ in range(2)]
    assert C.cursor <= ARENA_BYTES, C.cursor
    murowK = murowE
    mubK = mubE
    rstdK = rstdbE
    cenbK = cenbE

    for dst, src in ((s_cosq, cosq), (s_sinq, sinq), (s_cosk, cosk), (s_sink, sink),
                     (s_selm, selmid), (s_sel204, sel204), (s_urow, urow)):
        p.dma("gpsimd", dst[:, :], src[:, :], sem="gc2")
    tables_g = p.ct["gc2"]

    wload_ct = {}
    proj_pe = {}

    def emit_wload(h):
        pw = h % 2
        p.wait("gpsimd", "pe", proj_pe.get(h - 2, 0))
        p.dma("gpsimd", wqs[pw][:, :], wq[h, :, :], sem=f"wp{pw}")
        p.dma("gpsimd", wks[pw][:, :], wk[h, :, :], sem=f"wp{pw}")
        wload_ct[h] = p.dma("gpsimd", wvs[pw][:, :], wv[h, :, :], sem=f"wp{pw}")

    emit_wload(0)
    emit_wload(1)
    vslot_dv = {k: 0 for k in range(8)}
    reader_dv = {0: 0, 1: 0}
    wload_2_sentinel = None

    p.wait("tensor", "gc1", const_g)
    p.wait("scalar", "gc1", const_g)
    p.wait("vector", "gc1", const_g)
    stg_dv = [0, 0]
    l1pe = {}
    for c in range(NCH):
        bi = c % 2
        p.wait("sync", "gp", stg_dv[bi])
        p.wait("sync", "pe", l1pe.get(c - 2, 0))
        ssem = ("stg0", "stg1")[bi]
        gct = p.dma("sync", xst[bi][:, :], xqT[c * 128:(c + 1) * 128, :], sem=ssem)
        p.wait("gpsimd", ssem, gct)
        gpc = p.op("gpsimd", lambda E, b=bi: E.tensor_tensor(
            sqst[b][:, :], xst[b][:, :], xst[b][:, :], ALU.mult), inc="gp")
        p.wait("tensor", ssem, gct)
        p.wait("tensor", "gp", gpc)
        for hf in range(2):
            p.op("tensor", lambda E, b=bi, hf=hf, c=c: E.matmul(
                pB[0:1, hf * 512:(hf + 1) * 512], s_ones_c[:, :],
                xst[b][:, hf * 512:(hf + 1) * 512],
                start=(c == 0), stop=(c == NCH - 1)))
            l1pe[c] = p.op("tensor", lambda E, b=bi, hf=hf, c=c: E.matmul(
                pC[0:1, hf * 512:(hf + 1) * 512], s_ones_c[:, :],
                sqst[b][:, hf * 512:(hf + 1) * 512],
                start=(c == 0), stop=(c == NCH - 1)), inc=("pe" if hf == 1 else None))
        stg_dv[bi] = gpc
    pec = l1pe[NCH - 1]
    p.wait("vector", "pe", pec)
    for hf in range(2):
        p.op("vector", lambda E, hf=hf: E.tensor_scalar(
            murowE[:, hf * 512:(hf + 1) * 512], pB[0:1, hf * 512:(hf + 1) * 512],
            1.0 / DM, None, ALU.mult))
    dvc = p.op("vector", lambda E: E.nop(), inc="dv")
    p.wait("tensor", "dv", dvc)
    for hf in range(2):
        bcpe = p.op("tensor", lambda E, hf=hf: E.matmul(
            pB[:, hf * 512:(hf + 1) * 512], s_ones_r[:, :],
            murowE[:, hf * 512:(hf + 1) * 512], start=True, stop=True),
            inc=("pe" if hf == 1 else None))
    p.wait("vector", "pe", bcpe)
    p.op("vector", lambda E: E.tensor_copy(mubE[:, :], pB[:, :]))
    p.op("vector", lambda E: E.tensor_scalar(
        murowE[:, 0:512], pC[0:1, 0:512], 1.0 / DM, None, ALU.mult))
    dvc = p.op("vector", lambda E: E.tensor_scalar(
        murowE[:, 512:1024], pC[0:1, 512:1024], 1.0 / DM, None, ALU.mult), inc="dv")
    p.wait("tensor", "dv", dvc)
    for hf in range(2):
        p.op("tensor", lambda E, hf=hf: E.matmul(
            pC[:, hf * 512:(hf + 1) * 512], s_ones_r[:, :],
            murowE[:, hf * 512:(hf + 1) * 512], start=True,
            stop=True), inc=("pe" if hf == 1 else None))
    bcpe2 = p.ct["pe"]
    p.wait("vector", "pe", bcpe2)
    p.op("vector", lambda E: E.tensor_tensor(cenbE[:, :], mubE[:, :], mubE[:, :], ALU.mult))
    dvc = p.op("vector", lambda E: E.tensor_tensor(
        cenbE[:, :], pC[:, :], cenbE[:, :], ALU.subtract), inc="dv")
    p.wait("scalar", "dv", dvc)
    p.op("scalar", lambda E: E.activation(rstdbE[:, :], cenbE[:, :], AF.Ln, bias=s_eps[:, 0:1]))
    acc = p.op("scalar", lambda E: E.activation(rstdbE[:, :], rstdbE[:, :], AF.Exp, scale=-0.5), inc="ac")
    p.wait("vector", "ac", acc)
    qfin_dv = dvc

    p.wait("sync", "pe", l1pe[NCH - 1])
    p.wait("sync", "gp", p.ct["gp"])
    p.wait("tensor", "wp0", wload_ct[0])
    p.wait("vector", "gc2", tables_g)
    p.wait("tensor", "gc2", tables_g)
    p.wait("tensor", "dv", qfin_dv)
    stg_dv = [0, 0]
    kst_dv = [0, 0]
    kpe = {}
    qapply_dv = {}
    qpj_pe = {}
    for c in range(NCH):
        bi = c % 2
        p.wait("sync", "dv", stg_dv[bi])
        ssem = ("stg0", "stg1")[bi]
        gct = p.dma("sync", xst[bi][:, :], xqT[c * 128:(c + 1) * 128, :], sem=ssem)
        p.wait("vector", ssem, gct)
        p.op("vector", lambda E, b=bi: E.tensor_tensor(
            cenbE[:, :], xst[b][:, :], mubE[:, :], ALU.subtract))
        stg_dv[bi] = qapply_dv[c] = p.op("vector", lambda E, b=bi, c=c: E.tensor_tensor(
            hsqT[:, c * 1024:(c + 1) * 1024], cenbE[:, :], rstdbE[:, :], ALU.mult), inc="dv")
        p.wait("gpsimd", "gp", kst_dv[bi])
        p.wait("gpsimd", "pe", kpe.get(c - 2, 0))
        ksem = ("stg2", "stg3")[bi]
        kct = p.dma("gpsimd", xst2[bi][:, :], xkT[c * 128:(c + 1) * 128, :], sem=ksem)
        p.wait("gpsimd", ksem, kct)
        kgp = p.op("gpsimd", lambda E, b=bi: E.tensor_tensor(
            sqst2[b][:, :], xst2[b][:, :], xst2[b][:, :], ALU.mult), inc="gp")
        kst_dv[bi] = kgp
        p.wait("tensor", ksem, kct)
        p.wait("tensor", "gp", kgp)
        for hf in range(2):
            p.op("tensor", lambda E, b=bi, hf=hf, c=c: E.matmul(
                pB[0:1, hf * 512:(hf + 1) * 512], s_ones_c[:, :],
                xst2[b][:, hf * 512:(hf + 1) * 512],
                start=(c == 0), stop=(c == NCH - 1)))
            kpe[c] = p.op("tensor", lambda E, b=bi, hf=hf, c=c: E.matmul(
                pA[0:1, hf * 512:(hf + 1) * 512], s_ones_c[:, :],
                sqst2[b][:, hf * 512:(hf + 1) * 512],
                start=(c == 0), stop=(c == NCH - 1)), inc=("pe" if hf == 1 else None))
        p.wait("tensor", "dv", qapply_dv[c])
        for hf in range(2):
            qpj_pe[(c, hf)] = p.op("tensor", lambda E, hf=hf, c=c: E.matmul(
                pC[:, hf * 512:(hf + 1) * 512],
                wqs[0][:, c * 128:(c + 1) * 128],
                hsqT[:, c * 1024 + hf * 512: c * 1024 + hf * 512 + 512],
                start=(c == 0), stop=(c == NCH - 1)),
                inc=("pe" if c == NCH - 1 else None))

    def rope_evac(dst, tc, ts_, hf, pr, pec, do_rope=True):
        p.wait("vector", "pe", pec)
        s0 = hf * 512
        if do_rope:
            p.op("vector", lambda E: E.tensor_tensor(
                dst[:, s0:s0 + 512], pr[:, s0:s0 + 512], tc[:, s0:s0 + 512], ALU.mult))
            p.op("vector", lambda E: E.tensor_tensor(
                rtmp[0:64, 0:512], pr[64:128, s0:s0 + 512], ts_[0:64, s0:s0 + 512], ALU.mult))
            p.op("vector", lambda E: E.tensor_tensor(
                rtmp[64:128, 0:512], pr[0:64, s0:s0 + 512], ts_[64:128, s0:s0 + 512], ALU.mult))
            return p.op("vector", lambda E: E.tensor_tensor(
                dst[:, s0:s0 + 512], dst[:, s0:s0 + 512], rtmp[:, 0:512], ALU.add), inc="dv")
        return p.op("vector", lambda E: E.tensor_copy(
            dst[:, s0:s0 + 512], pr[:, s0:s0 + 512]), inc="dv")

    for hf in range(2):
        reader_dv[hf] = rope_evac(qTt[0], s_cosq, s_sinq, hf, pC, qpj_pe[(NCH - 1, hf)])

    pec = kpe[NCH - 1]
    p.wait("vector", "pe", pec)
    for hf in range(2):
        p.op("vector", lambda E, hf=hf: E.tensor_scalar(
            murowK[:, hf * 512:(hf + 1) * 512], pB[0:1, hf * 512:(hf + 1) * 512],
            1.0 / DM, None, ALU.mult))
    dvc = p.op("vector", lambda E: E.nop(), inc="dv")
    p.wait("tensor", "dv", dvc)
    for hf in range(2):
        bcpe = p.op("tensor", lambda E, hf=hf: E.matmul(
            pB[:, hf * 512:(hf + 1) * 512], s_ones_r[:, :],
            murowK[:, hf * 512:(hf + 1) * 512], start=True, stop=True),
            inc=("pe" if hf == 1 else None))
    p.wait("vector", "pe", bcpe)
    p.op("vector", lambda E: E.tensor_copy(mubK[:, :], pB[:, :]))
    p.op("vector", lambda E: E.tensor_scalar(
        murowK[:, 0:512], pA[0:1, 0:512], 1.0 / DM, None, ALU.mult))
    dvc = p.op("vector", lambda E: E.tensor_scalar(
        murowK[:, 512:1024], pA[0:1, 512:1024], 1.0 / DM, None, ALU.mult), inc="dv")
    p.wait("tensor", "dv", dvc)
    for hf in range(2):
        p.op("tensor", lambda E, hf=hf: E.matmul(
            pA[:, hf * 512:(hf + 1) * 512], s_ones_r[:, :],
            murowK[:, hf * 512:(hf + 1) * 512], start=True,
            stop=True), inc=("pe" if hf == 1 else None))
    bcpe2 = p.ct["pe"]
    p.wait("vector", "pe", bcpe2)
    p.op("vector", lambda E: E.tensor_tensor(cenbK[:, :], mubK[:, :], mubK[:, :], ALU.mult))
    dvc = p.op("vector", lambda E: E.tensor_tensor(
        cenbK[:, :], pA[:, :], cenbK[:, :], ALU.subtract), inc="dv")
    pA_free_dv = dvc
    p.wait("scalar", "dv", dvc)
    p.op("scalar", lambda E: E.activation(rstdK[:, :], cenbK[:, :], AF.Ln, bias=s_eps[:, 0:1]))
    acc = p.op("scalar", lambda E: E.activation(rstdK[:, :], rstdK[:, :], AF.Exp, scale=-0.5), inc="ac")
    p.wait("vector", "ac", acc)

    p.wait("gpsimd", "pe", kpe[NCH - 1])
    p.wait("gpsimd", "gp", p.ct["gp"])
    kst_dv = [0, 0]
    kapply_dv = {}
    kpj_pe = {}
    p.wait("tensor", "dv", reader_dv[1])
    for c in range(NCH):
        bi = c % 2
        p.wait("scalar", "dv", kst_dv[bi])
        ksem = ("stg2", "stg3")[bi]
        kct = p.dma("scalar", xst2[bi][:, :], xkT[c * 128:(c + 1) * 128, :], sem=ksem)
        p.wait("vector", ksem, kct)
        p.op("vector", lambda E, b=bi: E.tensor_tensor(
            cenbK[:, :], xst2[b][:, :], mubK[:, :], ALU.subtract))
        kst_dv[bi] = kapply_dv[c] = p.op("vector", lambda E, b=bi, c=c: E.tensor_tensor(
            hskT[:, c * 1024:(c + 1) * 1024], cenbK[:, :], rstdK[:, :], ALU.mult), inc="dv")
        p.wait("tensor", "dv", kapply_dv[c])
        for hf in range(2):
            kpj_pe[(c, hf)] = p.op("tensor", lambda E, hf=hf, c=c: E.matmul(
                pC[:, hf * 512:(hf + 1) * 512],
                wks[0][:, c * 128:(c + 1) * 128],
                hskT[:, c * 1024 + hf * 512: c * 1024 + hf * 512 + 512],
                start=(c == 0), stop=(c == NCH - 1)),
                inc=("pe" if c == NCH - 1 else None))
    for hf in range(2):
        reader_dv[hf] = rope_evac(kTt[0], s_cosk, s_sink, hf, pC, kpj_pe[(NCH - 1, hf)])

    p.wait("tensor", "dv", reader_dv[1])
    for hf in range(2):
        vpe = None
        for c in range(NCH):
            vpe = p.op("tensor", lambda E, hf=hf, c=c: E.matmul(
                pC[:, hf * 512:(hf + 1) * 512],
                wvs[0][:, c * 128:(c + 1) * 128],
                hskT[:, c * 1024 + hf * 512: c * 1024 + hf * 512 + 512],
                start=(c == 0), stop=(c == NCH - 1)), inc=("pe" if c == NCH - 1 else None))
        reader_dv[hf] = rope_evac(vft[0], None, None, hf, pC, vpe, do_rope=False)
    proj_pe[0] = p.ct["pe"]
    emit_wload(2)
    p.wait("tensor", "dv", reader_dv[1])
    for kt in range(8):
        sl = kt % 2
        ptb = (pT1, pT2)[sl]
        p.wait("tensor", "dv", vslot_dv[sl])
        pec = p.op("tensor", lambda E, kt=kt, ptb=ptb: E.transpose(
            ptb[:, 0:128], vft[0][:, kt * 128:(kt + 1) * 128],
            s_idb[:, :]), inc="pe")
        p.wait("vector", "pe", pec)
        vslot_dv[sl] = p.op("vector", lambda E, kt=kt, ptb=ptb: E.tensor_copy(
            vtok[0][:, kt * 128:(kt + 1) * 128], ptb[:, 0:128]), inc="dv")
    entry_done_dv = p.ct["dv"]
    entry_done_gp = p.ct["gp"]
    pC_free_dv = reader_dv[1]

    p.wait("sync", "pe", p.ct["pe"])
    for rr in range(32):
        p.op("sync", lambda E, rr=rr: E.dma_start(
            out=disp2[rr * 128:(rr + 1) * 128, :], in_=zrows[:, :]), inc="s", k=16)
    d2z_ct = p.ct["s"]

    pass2_pe_prev = 0
    cb_pe_prev = 0
    crx_prev = 0
    prev_pv_dv = entry_done_dv

    def genA(h):
        pw = h % 2
        ix = h % 2
        mats = ((wqs[pw], hsqT, qTt[ix], s_cosq, s_sinq, True),
                (wks[pw], hskT, kTt[ix], s_cosk, s_sink, True),
                (wvs[pw], hskT, vft[ix], None, None, False))
        first = True
        for (wsl, hsT, dst, tc, ts_, do_rope) in mats:
            for hf in range(2):
                bk = hf
                if first:
                    p.wait("tensor", f"wp{pw}", wload_ct[h])
                    if h == 0:
                        p.wait("tensor", "gc2", tables_g)
                        p.wait("vector", "gc2", tables_g)
                    first = False
                p.wait("tensor", "dv", reader_dv[bk])
                for c in range(NCH):
                    pec = p.op("tensor", lambda E, w=wsl, hs=hsT, hf=hf, c=c, bk=bk: E.matmul(
                        pB[:, bk * 512:(bk + 1) * 512],
                        w[:, c * 128:(c + 1) * 128],
                        hs[:, c * 1024 + hf * 512: c * 1024 + hf * 512 + 512],
                        start=(c == 0), stop=(c == NCH - 1)), inc=("pe" if c == NCH - 1 else None))
                p.wait("vector", "pe", pec)
                s0 = hf * 512
                if do_rope:
                    p.op("vector", lambda E, dst=dst, tc=tc, s0=s0, bk=bk: E.tensor_tensor(
                        dst[:, s0:s0 + 512], pB[:, bk * 512:bk * 512 + 512],
                        tc[:, s0:s0 + 512], ALU.mult))
                    p.op("vector", lambda E, ts_=ts_, s0=s0, bk=bk: E.tensor_tensor(
                        rtmp[0:64, 0:512], pB[64:128, bk * 512:bk * 512 + 512],
                        ts_[0:64, s0:s0 + 512], ALU.mult))
                    p.op("vector", lambda E, ts_=ts_, s0=s0, bk=bk: E.tensor_tensor(
                        rtmp[64:128, 0:512], pB[0:64, bk * 512:bk * 512 + 512],
                        ts_[64:128, s0:s0 + 512], ALU.mult))
                    reader_dv[bk] = p.op("vector", lambda E, dst=dst, s0=s0: E.tensor_tensor(
                        dst[:, s0:s0 + 512], dst[:, s0:s0 + 512], rtmp[:, 0:512], ALU.add), inc="dv")
                else:
                    reader_dv[bk] = p.op("vector", lambda E, dst=dst, s0=s0, bk=bk: E.tensor_copy(
                        dst[:, s0:s0 + 512], pB[:, bk * 512:bk * 512 + 512]), inc="dv")
                yield
        proj_pe[h] = p.ct["pe"]
        if h + 2 < nh:
            emit_wload(h + 2)
        p.wait("tensor", "dv", reader_dv[1])
        for kt in range(8):
            sl = kt % 2
            ptb = (pT1, pT2)[sl]
            p.wait("tensor", "dv", vslot_dv[sl])
            pec = p.op("tensor", lambda E, kt=kt, ptb=ptb, ix=ix: E.transpose(
                ptb[:, 0:128], vft[ix][:, kt * 128:(kt + 1) * 128],
                s_idb[:, :]), inc="pe")
            p.wait("vector", "pe", pec)
            vslot_dv[sl] = p.op("vector", lambda E, kt=kt, ptb=ptb, ix=ix: E.tensor_copy(
                vtok[ix][:, kt * 128:(kt + 1) * 128], ptb[:, 0:128]), inc="dv")
            if kt == 3:
                yield
        yield

    def genB(h):
        nonlocal pA_free_dv, pC_free_dv, pass2_pe_prev, prev_pv_dv, cb_pe_prev, crx_prev
        ix = h % 2
        qT = qTt[ix]
        kT = kTt[ix]
        vT = vtok[ix]
        pass1_ac = {}
        pass1_cp = {}
        for qt in range(8):
            pr = (pA, pC)[qt % 2]
            ev = p1ev[qt % 2]
            if qt == 0:
                p.wait("tensor", "dv", reader_dv[1])
                p.wait("tensor", "dv", pA_free_dv)
            elif qt == 1:
                p.wait("tensor", "dv", pC_free_dv)
            else:
                p.wait("tensor", "dv", pass1_cp[qt - 2])
            for hf in range(2):
                pec = p.op("tensor", lambda E, qt=qt, hf=hf, qT=qT, kT=kT, pr=pr: E.matmul(
                    pr[:, hf * 512:(hf + 1) * 512],
                    qT[:, qt * 128:(qt + 1) * 128],
                    kT[:, hf * 512:(hf + 1) * 512],
                    start=True, stop=True), inc=("pe" if hf == 1 else None))
            p.wait("vector", "pe", pec)
            if qt >= 2:
                p.wait("vector", "ac", pass1_ac[qt - 2])
            pass1_cp[qt] = p.op("vector", lambda E, pr=pr, ev=ev: E.tensor_copy(
                ev[:, 0:1023], pr[:, 0:1023]), inc="dv")
            dvc = p.op("vector", lambda E, ev=ev, qt=qt: E.reduce_max(
                sstM[:, qt:qt + 1], ev[:, 0:1023], axis=AX, negate=True), inc="dv")
            p.wait("scalar", "dv", dvc)
            p.op("scalar", lambda E, ev=ev, qt=qt: E.activation(
                junk[:, 0:1023], ev[:, 0:1023], AF.Exp, bias=sstM[:, qt:qt + 1],
                accum_out=sst[:, 4:5]))
            acc = p.op("scalar", lambda E, qt=qt: E.activation(
                sstL[:, qt:qt + 1], sst[:, 4:5], AF.Ln), inc="ac")
            pass1_ac[qt] = acc
            yield
        p.wait("vector", "ac", pass1_ac[7])
        last_stt = p.op("vector", lambda E: E.scalar_tensor_tensor(
            stq[:, 0:8], sstM[:, 0:8], 1.0, sstL[:, 0:8], ALU.mult, ALU.subtract), inc="dv")
        pass2_ac = {}
        for kt in (0, 1):
            pr = (pA, pC)[kt % 2]
            p.wait("tensor", "dv", pass1_cp[6 + kt])
            for hf in range(2):
                p.op("tensor", lambda E, kt=kt, hf=hf, qT=qT, kT=kT, pr=pr: E.matmul(
                    pr[:, hf * 512:(hf + 1) * 512],
                    kT[:, kt * 128:(kt + 1) * 128],
                    qT[:, hf * 512:(hf + 1) * 512],
                    start=True, stop=True), inc=("pe" if hf == 1 else None))
            pec = p.ct["pe"]
            p.wait("scalar", "pe", pec)
            pass2_ac[kt] = p.op("scalar", lambda E, kt=kt, pr=pr: E.activation(
                probsT[:, kt * 1024:(kt + 1) * 1024], pr[:, :], AF.Exp), inc="ac")
            yield
        p.wait("tensor", "dv", last_stt)
        p.wait("tensor", "ac", pass2_ac[0])
        pec = p.op("tensor", lambda E: E.transpose(pA[0:8, 0:128], stq[:, 0:8], s_idf[:, :]), inc="pe")
        p.wait("vector", "pe", pec)
        dvc = p.op("vector", lambda E: E.tensor_copy(stqT[:, :], pA[0:8, 0:128]), inc="dv")
        p.wait("gpsimd", "dv", dvc)
        p.wait("gpsimd", "pe", pass2_pe_prev)
        p.wait("gpsimd", "ac", crx_prev)
        ngc = p.dma("gpsimd", negrow[:, :], stqT[:, :], sem="gs")
        p.wait("scalar", "gs", ngc)
        p.wait("scalar", "pe", cb_pe_prev)
        crx = p.op("scalar", lambda E: E.activation(
            negrh[:, :], negrow[:, :], AF.Exp), inc="ac")
        crx_prev = crx
        stats_dv = dvc
        yield
        for kt in range(2, 8):
            pr = (pA, pC)[kt % 2]
            if kt == 2:
                p.wait("tensor", "dv", dvc)
            else:
                p.wait("tensor", "ac", pass2_ac[kt - 2])
            for hf in range(2):
                p.op("tensor", lambda E, kt=kt, hf=hf, qT=qT, kT=kT, pr=pr: E.matmul(
                    pr[:, hf * 512:(hf + 1) * 512],
                    kT[:, kt * 128:(kt + 1) * 128],
                    qT[:, hf * 512:(hf + 1) * 512],
                    start=True, stop=True), inc=("pe" if hf == 1 else None))
            pec = p.ct["pe"]
            p.wait("scalar", "pe", pec)
            acc = p.op("scalar", lambda E, kt=kt, pr=pr: E.activation(
                probsT[:, kt * 1024:(kt + 1) * 1024], pr[:, :], AF.Exp), inc="ac")
            pass2_ac[kt] = acc
            yield
        pass2_pe_prev = p.ct["pe"]
        p.wait("tensor", "ac", pass2_ac[7])
        p.wait("tensor", "dv", stats_dv)
        for kt in range(8):
            p.op("tensor", lambda E, kt=kt: E.matmul(
                pA[0:1, 0:1], s_selm[:, kt:kt + 1],
                probsT[:, kt * 1024 + 1023: kt * 1024 + 1024],
                start=(kt == 0), stop=(kt == 7)))
        p.op("tensor", lambda E: E.matmul(
            pA[0:1, 1:2], s_sel204[:, :], probsT[:, 1 * 1024 + 1023: 1 * 1024 + 1024],
            start=True, stop=True))
        p.op("tensor", lambda E, vT=vT: E.matmul(
            pA[0:1, 128:256], s_sel204[:, :], vT[:, 1 * 128: 2 * 128], start=True, stop=True))
        for kt in range(8):
            pec = p.op("tensor", lambda E, kt=kt: E.matmul(
                pA[0:1, 256:512], s_selm[:, kt:kt + 1],
                probsT[:, kt * 1024 + LAST_LOC: kt * 1024 + 1024],
                start=(kt == 0), stop=(kt == 7)), inc=("pe" if kt == 7 else None))
        p.wait("vector", "pe", pec)
        p.op("vector", lambda E: E.tensor_copy(evs[:, 6:7], negrh[0:1, 1023:1024]))
        p.op("vector", lambda E: E.tensor_scalar(
            evs[:, 0:1], pA[0:1, 0:1], evs[:, 6:7], None, ALU.mult))
        p.op("vector", lambda E: E.tensor_scalar(
            evs[:, 0:1], evs[:, 0:1], 1.0 / 818.0, 1e-6, ALU.mult, ALU.add))
        p.op("vector", lambda E: E.reciprocal(evs[:, 1:2], evs[:, 0:1]))
        p.op("vector", lambda E: E.tensor_scalar(
            evs[:, 7:8], pA[0:1, 1:2], evs[:, 6:7], None, ALU.mult))
        p.op("vector", lambda E: E.tensor_tensor(evs[:, 2:3], evs[:, 7:8], evs[:, 1:2], ALU.mult))
        p.op("vector", lambda E: E.tensor_scalar_min(evs[:, 3:4], evs[:, 2:3], 1.0))
        p.op("vector", lambda E, h=h: E.tensor_tensor(
            evs[:, 4:5], s_urow[:, h:h + 1], evs[:, 3:4], ALU.is_lt))
        p.op("vector", lambda E: E.tensor_scalar_mul(evs[:, 5:6], evs[:, 4:5], 1.0 / RB))
        p.op("vector", lambda E: E.tensor_scalar(
            vero[:, :], pA[0:1, 128:256], evs[:, 5:6], None, ALU.mult))
        dvc = p.op("vector", lambda E: E.tensor_copy(srowb[:, :], pA[0:1, 256:512]), inc="dv")
        pA_free_dv = dvc
        yield
        p.wait("tensor", "dv", dvc)
        for hf in range(2):
            for kt in range(8):
                p.op("tensor", lambda E, hf=hf, kt=kt, vT=vT: E.matmul(
                    pC[:, hf * 512:(hf + 1) * 512],
                    vT[:, kt * 128:(kt + 1) * 128],
                    probsT[:, kt * 1024 + hf * 512: kt * 1024 + hf * 512 + 512],
                    start=(kt == 0), stop=(kt == 7 and hf == 0)))
        pec = p.op("tensor", lambda E: E.matmul(
            pC[:, 768:1024], vero[:, :], srowb[:, :], start=False, stop=True), inc="pe")
        p.wait("tensor", "ac", crx)
        p.op("tensor", lambda E: E.matmul(
            pA[:, 0:512], s_ones_rb[:, :], negrh[0:1, 0:512], start=True, stop=True))
        cbpe = p.op("tensor", lambda E: E.matmul(
            pA[:, 512:1024], s_ones_rb[:, :], negrh[0:1, 512:1024],
            start=True, stop=True), inc="pe")
        cb_pe_prev = cbpe
        p.wait("vector", "pe", cbpe)
        cbdv = p.op("vector", lambda E: E.tensor_copy(cbs[:, :], pA[:, 0:1024]), inc="dv")
        pA_free_dv = cbdv
        p.op("vector", lambda E, h=h: E.tensor_tensor(
            aoT[:, h * 1024:h * 1024 + 512], pC[:, 0:512], cbs[:, 0:512], ALU.mult))
        prev_pv_dv = p.op("vector", lambda E, h=h: E.tensor_tensor(
            aoT[:, h * 1024 + 512:(h + 1) * 1024], pC[:, 512:1024],
            cbs[:, 512:1024], ALU.mult), inc="dv")
        pC_free_dv = prev_pv_dv
        yield

    if nh > 0:
        for h in range(nh):
            b = genB(h)
            a2 = genA(h + 1) if h + 1 < nh else None
            i = 0
            for _ in b:
                i += 1
                if a2 is not None and i in (3, 6, 8, 9, 12, 15, 17, 18):
                    next(a2, None)
            if a2 is not None:
                for _ in a2:
                    pass

    C.at(GLOB_END)
    XT = C.f32(16 * 1024)

    wold = {}

    def emit_woload(oc):
        pwv = oc % 2
        p.wait("gpsimd", "pe", proj_pe.get(nh - 2 + pwv, 0) if oc < 2 else wo_pe[oc - 2])
        wold[oc] = p.dma("gpsimd", wqs[pwv][:, :], wo[oc, :, :], sem=f"wp{pwv}")

    if do_w:
        emit_woload(0)
        emit_woload(1)
    p.wait("gpsimd", "pe", proj_pe.get(nh - 1, 0))
    xt_g = 0
    for c in range(NCH if do_w else 0):
        xt_g = p.dma("gpsimd", XT[:, c * 1024:(c + 1) * 1024],
                     xqT[c * 128:(c + 1) * 128, :], sem="stg0")
    woev = {}
    wo_pe = {}
    for oc in range(16 if do_w else 0):
        pwv = oc % 2
        p.wait("tensor", f"wp{pwv}", wold[oc])
        if oc == 0:
            p.wait("tensor", "dv", prev_pv_dv)
        pp = pB if pwv == 0 else pC
        for hf in range(2):
            if oc >= 2:
                p.wait("tensor", "dv", woev[(oc - 2, hf)])
            for c in range(NCH):
                pec = p.op("tensor", lambda E, pp=pp, pwv=pwv, hf=hf, c=c: E.matmul(
                    pp[:, hf * 512:(hf + 1) * 512],
                    wqs[pwv][:, c * 128:(c + 1) * 128],
                    aoT[:, c * 1024 + hf * 512: c * 1024 + hf * 512 + 512],
                    start=(c == 0), stop=(c == NCH - 1)), inc=("pe" if c == NCH - 1 else None))
            p.wait("vector", "pe", pec)
            if oc == 0 and hf == 0:
                p.wait("vector", "stg0", xt_g)
            woev[(oc, hf)] = p.op("vector", lambda E, pp=pp, oc=oc, hf=hf: E.tensor_tensor(
                XT[:, oc * 1024 + hf * 512: oc * 1024 + hf * 512 + 512],
                pp[:, hf * 512:(hf + 1) * 512],
                XT[:, oc * 1024 + hf * 512: oc * 1024 + hf * 512 + 512],
                ALU.add), inc="dv")
        wo_pe[oc] = p.ct["pe"]
        if oc + 2 < 16:
            emit_woload(oc + 2)
    wo_done_dv = p.ct["dv"]

    C.at(A_SCRATCH)
    hs2T = C.b16(16 * 1024)
    Z1_END = C.cursor
    w2slot = C.b16(SLAB)
    wbuf = [C.b16(2048) for _ in range(4)]
    dwT = C.b16(1024, p0=0, np_=16)
    s_gateT = C.f32(256)
    s_e4 = C.f32(1, p0=0, np_=16)
    idxsb = C.i32(64)
    wecol = [C.f32(2) for _ in range(2)]
    mcols = C.f32(2)
    slot1b = C.i32(8)
    slot2b = C.i32(8)
    OV = C.cursor
    sq2 = [C.f32(1024) for _ in range(2)]
    murow = C.f32(1024, p0=0, np_=1)
    msqrow = C.f32(1024, p0=0, np_=1)
    mub = C.f32(1024)
    rstdb = C.f32(1024)
    cenb = C.f32(1024)
    xfc = [C.f32(1024) for _ in range(2)]
    sg = C.f32(128)
    dwt = C.f32(128)
    tk = C.f32(16)
    tcols = C.f32(8)
    L2_END = C.cursor
    assert L2_END <= ARENA_BYTES, L2_END
    C.at(OV)
    sct = [C.f32(1024, p0=0, np_=16) for _ in range(3)]
    sgT = sct[0]
    sctT = C.f32(16)
    sctT2 = C.f32(16)
    IDXB_END = C.cursor
    assert IDXB_END <= ARENA_BYTES
    C.at(OV)
    h_st = C.b16(ICN * 1024)
    yst = [C.f32(2048) for _ in range(2)]
    SH_END = C.cursor
    assert SH_END <= ARENA_BYTES
    C.at(A_SCRATCH)
    disp = [[C.b16(DM + 16), C.b16(DM + 16)] for _ in range(2)]
    dispT = C.b16(16 * 256)
    h_sp = C.b16(ICN * 256)
    RT_END = C.cursor
    assert RT_END <= Z1_END, (RT_END, Z1_END)
    C.at(IDXB_END)
    hstage = [C.b16(DM + 16) for _ in range(2)]
    HST_END = C.cursor
    assert HST_END <= ARENA_BYTES
    C.at(OV)
    mstage = [C.b16(2048) for _ in range(8)]

    p.dma("gpsimd", s_gateT[:, :], gateT[:, :], sem="gc3")
    gate_g = p.dma("gpsimd", s_e4[:, :], e4col[:, :], sem="gc3")

    w2_ct = {}
    d_stage_pe = {}

    def emit_w2dma(ve):
        p.wait("gpsimd", "pe", d_stage_pe.get(ve - 1, 0))
        p.wait("gpsimd", "dv", wo_done_dv)
        w2_ct[ve] = p.dma("gpsimd", w2slot[:, :], w2v[ve, :, :], sem="w2s")

    emit_w2dma(0)

    blk_ct = {}
    buf_free_pe = [0, 0, 0, 0]

    def bufix(ve, blk):
        return (ve * 2 * ICN + blk) % 4

    def emit_blkdma(ve, blk):
        mat = w1v if blk < ICN else w3v
        ic = blk % ICN
        bx = bufix(ve, blk)
        p.wait("sync", "pe", buf_free_pe[bx])
        if ve == 0 and blk < 4:
            p.wait("sync", "dv", wo_done_dv)
        blk_ct[(ve, blk)] = p.dma("sync", wbuf[bx][:, :],
                                  mat[ve, :, ic * 2048:(ic + 1) * 2048], sem=f"wb{bx}")

    _saved_q = None
    if not do_l2:
        _saved_q = {k: len(v) for k, v in p.q.items()}
        _saved_ct = dict(p.ct)
    p.wait("gpsimd", "dv", wo_done_dv)
    p.wait("tensor", "dv", wo_done_dv)
    l2a_pe = {}
    for c in range(NCH):
        if c >= 2:
            p.wait("gpsimd", "pe", l2a_pe[c - 2])
        gpc = p.op("gpsimd", lambda E, c=c, b=c % 2: E.tensor_tensor(
            sq2[b][:, :], XT[:, c * 1024:(c + 1) * 1024],
            XT[:, c * 1024:(c + 1) * 1024], ALU.mult), inc="gp")
        p.wait("tensor", "gp", gpc)
        for hf in range(2):
            p.op("tensor", lambda E, c=c, hf=hf: E.matmul(
                pB[0:1, hf * 512:(hf + 1) * 512], s_ones_c[:, :],
                XT[:, c * 1024 + hf * 512: c * 1024 + hf * 512 + 512],
                start=(c == 0), stop=(c == NCH - 1)))
            l2a_pe[c] = p.op("tensor", lambda E, c=c, hf=hf, b=c % 2: E.matmul(
                pC[0:1, hf * 512:(hf + 1) * 512], s_ones_c[:, :],
                sq2[b][:, hf * 512:(hf + 1) * 512],
                start=(c == 0), stop=(c == NCH - 1)), inc=("pe" if hf == 1 else None))
    pec = l2a_pe[NCH - 1]
    p.wait("vector", "pe", pec)
    for hf in range(2):
        p.op("vector", lambda E, hf=hf: E.tensor_scalar(
            murow[:, hf * 512:(hf + 1) * 512], pB[0:1, hf * 512:(hf + 1) * 512],
            1.0 / DM, None, ALU.mult))
        p.op("vector", lambda E, hf=hf: E.tensor_scalar(
            msqrow[:, hf * 512:(hf + 1) * 512], pC[0:1, hf * 512:(hf + 1) * 512],
            1.0 / DM, None, ALU.mult))
    dvc = p.ct["dv"] + 0
    dvc = p.op("vector", lambda E: E.nop(), inc="dv")
    p.wait("tensor", "dv", dvc)
    for hf in range(2):
        p.op("tensor", lambda E, hf=hf: E.matmul(
            pB[:, hf * 512:(hf + 1) * 512], s_ones_r[:, :],
            murow[:, hf * 512:(hf + 1) * 512], start=True, stop=True))
        pec = p.op("tensor", lambda E, hf=hf: E.matmul(
            pC[:, hf * 512:(hf + 1) * 512], s_ones_r[:, :],
            msqrow[:, hf * 512:(hf + 1) * 512], start=True, stop=True),
            inc=("pe" if hf == 1 else None))
    p.wait("vector", "pe", pec)
    p.op("vector", lambda E: E.tensor_copy(mub[:, :], pB[:, :]))
    p.op("vector", lambda E: E.tensor_tensor(cenb[:, :], mub[:, :], mub[:, :], ALU.mult))
    dvc = p.op("vector", lambda E: E.tensor_tensor(
        cenb[:, :], pC[:, :], cenb[:, :], ALU.subtract), inc="dv")
    p.wait("scalar", "dv", dvc)
    p.op("scalar", lambda E: E.activation(rstdb[:, :], cenb[:, :], AF.Ln, bias=s_eps[:, 0:1]))
    acc = p.op("scalar", lambda E: E.activation(rstdb[:, :], rstdb[:, :], AF.Exp, scale=-0.5), inc="ac")
    p.wait("vector", "ac", acc)
    p.op("vector", lambda E: E.nop(), inc="dv")

    gate_pe = {}
    p.wait("tensor", "gc3", gate_g)
    for c in range(NCH):
        bi = c % 2
        if c >= 2:
            p.wait("vector", "pe", gate_pe[c - 2])
        p.op("vector", lambda E, c=c: E.tensor_tensor(
            cenb[:, :], XT[:, c * 1024:(c + 1) * 1024], mub[:, :], ALU.subtract))
        p.op("vector", lambda E, b=bi: E.tensor_tensor(
            xfc[b][:, :], cenb[:, :], rstdb[:, :], ALU.mult))
        dvc = p.op("vector", lambda E, c=c, b=bi: E.tensor_copy(
            hs2T[:, c * 1024:(c + 1) * 1024], xfc[b][:, :]), inc="dv")
        p.wait("tensor", "dv", dvc)
        for hf in range(2):
            gate_pe[c] = p.op("tensor", lambda E, c=c, hf=hf, b=bi: E.matmul(
                pA[0:16, hf * 512:(hf + 1) * 512],
                s_gateT[:, c * 16:(c + 1) * 16],
                xfc[b][:, hf * 512:(hf + 1) * 512],
                start=(c == 0), stop=(c == NCH - 1)), inc=("pe" if hf == 1 else None))
    p.wait("scalar", "pe", gate_pe[NCH - 1])
    acc = p.op("scalar", lambda E: E.activation(sgT[:, :], pA[0:16, :], AF.Sigmoid), inc="ac")
    p.wait("tensor", "ac", acc)
    SL4g = ((pB, 0), (pB, 512), (pC, 0), (pC, 512))
    gslot_dv = [0] * 4
    for qt in range(8):
        ppg, ofg = SL4g[qt % 4]
        p.wait("tensor", "dv", gslot_dv[qt % 4])
        pec = p.op("tensor", lambda E, qt=qt, ppg=ppg, ofg=ofg: E.transpose(
            ppg[:, ofg:ofg + 16], sgT[0:16, qt * 128:(qt + 1) * 128],
            s_idf[0:16, 0:16]), inc="pe")
        p.wait("vector", "pe", pec)
        gslot_dv[qt % 4] = p.op("vector", lambda E, qt=qt, ppg=ppg, ofg=ofg: E.tensor_copy(
            sg[:, qt * 16:(qt + 1) * 16], ppg[:, ofg:ofg + 16]), inc="dv")
    for qt in range(8):
        sgt = sg[:, qt * 16:(qt + 1) * 16]
        dwv = dwt[:, qt * 16:(qt + 1) * 16]
        p.op("vector", lambda E, sgt=sgt: E.reduce_max(tcols[:, 0:1], sgt, axis=AX))
        p.op("vector", lambda E, sgt=sgt: E.tensor_scalar(
            tk[:, :], sgt, tcols[:, 0:1], None, ALU.is_equal))
        p.op("vector", lambda E, sgt=sgt: E.scalar_tensor_tensor(
            tk[:, :], tk[:, :], -1e30, sgt, ALU.mult, ALU.add))
        p.op("vector", lambda E: E.reduce_max(tcols[:, 1:2], tk[:, :], axis=AX))
        p.op("vector", lambda E: E.tensor_tensor(tcols[:, 2:3], tcols[:, 0:1], tcols[:, 1:2], ALU.add))
        p.op("vector", lambda E: E.reciprocal(tcols[:, 3:4], tcols[:, 2:3]))
        p.op("vector", lambda E: E.tensor_tensor(tcols[:, 4:5], tcols[:, 0:1], tcols[:, 3:4], ALU.mult))
        p.op("vector", lambda E: E.tensor_tensor(tcols[:, 5:6], tcols[:, 1:2], tcols[:, 3:4], ALU.mult))
        p.op("vector", lambda E: E.tensor_scalar(
            tk[:, :], tk[:, :], tcols[:, 1:2], None, ALU.is_equal))
        p.op("vector", lambda E, sgt=sgt, dwv=dwv: E.tensor_scalar(
            dwv, sgt, tcols[:, 0:1], None, ALU.is_equal))
        p.op("vector", lambda E, dwv=dwv: E.tensor_scalar(
            dwv, dwv, tcols[:, 4:5], None, ALU.mult))
        p.op("vector", lambda E, dwv=dwv: E.scalar_tensor_tensor(
            dwv, tk[:, :], tcols[:, 5:6], dwv, ALU.mult, ALU.add), inc="dv")
    p.wait("tensor", "dv", p.ct["dv"])
    SL4d = ((pA, 0), (pA, 512), (pB, 0), (pB, 512))
    dslot_dv = [0] * 4
    for qt in range(8):
        ppd, ofd = SL4d[qt % 4]
        p.wait("tensor", "dv", dslot_dv[qt % 4])
        pec = p.op("tensor", lambda E, qt=qt, ppd=ppd, ofd=ofd: E.transpose(
            ppd[0:16, ofd:ofd + 128], dwt[:, qt * 16:(qt + 1) * 16],
            s_idf[:, :]), inc="pe")
        p.wait("vector", "pe", pec)
        dslot_dv[qt % 4] = p.op("vector", lambda E, qt=qt, ppd=ppd, ofd=ofd: E.tensor_copy(
            dwT[0:16, qt * 128:(qt + 1) * 128], ppd[0:16, ofd:ofd + 128]), inc="dv")
    dwT_dv = p.ct["dv"]

    if not do_l2:
        for k, n0 in _saved_q.items():
            del p.q[k][n0:]
        p.ct.update(_saved_ct)
    tslot = {0: ("dv", wo_done_dv), 1: ("dv", wo_done_dv)}
    TSL = ((pT1, 0), (pT2, 0))

    p.wait("vector", "gc3", gate_g)
    p.op("vector", lambda E: E.memset(sct[2][:, :], 0.0))
    p.op("vector", lambda E: E.tensor_single_scalar(sct[0][:, :], dwT[0:16, :], 0.0, ALU.is_gt))
    p.op("vector", lambda E: E.tensor_tensor_scan(
        sct[1][:, :], sct[0][:, :], sct[2][:, :], 0.0, ALU.add, ALU.add))
    p.op("vector", lambda E: E.tensor_scalar_add(sct[1][:, :], sct[1][:, :], -1.0))
    p.op("vector", lambda E: E.tensor_single_scalar(sct[2][:, :], sct[1][:, :], 255.5, ALU.is_le))
    p.op("vector", lambda E: E.tensor_tensor(sct[0][:, :], sct[0][:, :], sct[2][:, :], ALU.mult))
    p.op("vector", lambda E: E.tensor_scalar(
        sct[1][:, :], sct[1][:, :], s_e4[:, 0:1], None, ALU.add))
    p.op("vector", lambda E: E.tensor_scalar_add(sct[1][:, :], sct[1][:, :], -9000.0))
    p.op("vector", lambda E: E.tensor_tensor(sct[1][:, :], sct[0][:, :], sct[1][:, :], ALU.mult))
    slv = p.op("vector", lambda E: E.tensor_scalar_add(sct[1][:, :], sct[1][:, :], 9000.0), inc="dv")

    SL4s = ((pA, 0), (pA, 512), (pB, 0), (pB, 512))
    sslot_dv = [0] * 4
    for tt in range(8):
        pps, ofs = SL4s[tt % 4]
        p.wait("tensor", "dv", max(sslot_dv[tt % 4], slv))
        pec = p.op("tensor", lambda E, tt=tt, pps=pps, ofs=ofs: E.transpose(
            pps[:, ofs:ofs + 16], sct[1][0:16, tt * 128:(tt + 1) * 128],
            s_idf[0:16, 0:16]), inc="pe")
        p.wait("vector", "pe", pec)
        p.op("vector", lambda E, pps=pps, ofs=ofs: E.tensor_copy(sctT[:, :], pps[:, ofs:ofs + 16]))
        p.op("vector", lambda E: E.tensor_reduce(mcols[:, 0:1], sctT[:, :], AX, ALU.min))
        p.op("vector", lambda E: E.tensor_scalar(
            sctT2[:, :], sctT[:, :], mcols[:, 0:1], None, ALU.is_equal))
        p.op("vector", lambda E: E.scalar_tensor_tensor(
            sctT[:, :], sctT2[:, :], 9000.0, sctT[:, :], ALU.mult, ALU.add))
        p.op("vector", lambda E: E.tensor_reduce(mcols[:, 1:2], sctT[:, :], AX, ALU.min))
        p.op("vector", lambda E, tt=tt: E.tensor_copy(slot1b[:, tt:tt + 1], mcols[:, 0:1]))
        sslot_dv[tt % 4] = p.op("vector", lambda E, tt=tt: E.tensor_copy(
            slot2b[:, tt:tt + 1], mcols[:, 1:2]), inc="dv")
    slot_cast_dv = p.ct["dv"]

    dsc = {0: 0, 1: 0}
    p.wait("scalar", "dv", dwT_dv)
    for tt in range(8):
        bi = tt % 2
        dsem = ("hx", "st1")[bi]
        for c in range(NCH):
            sl = c % 2
            ptb, co = TSL[sl]
            p.wait("tensor", *tslot[sl])
            pec = p.op("tensor", lambda E, c=c, tt=tt, ptb=ptb, co=co: E.transpose(
                ptb[:, co:co + 128], hs2T[:, c * 1024 + tt * 128: c * 1024 + (tt + 1) * 128],
                s_idb[:, :]), inc="pe")
            p.wait("scalar", "pe", pec)
            if c == 0:
                p.wait("scalar", dsem, dsc[bi])
            tslot[sl] = ("ac", p.op("scalar", lambda E, c=c, b=bi, ptb=ptb, co=co: E.activation(
                hstage[b][:, c * 128:(c + 1) * 128], ptb[:, co:co + 128], AF.Copy), inc="ac"))
        dwc = p.op("scalar", lambda E, tt=tt, b=bi: E.activation(
            hstage[b][:, 2048:2064], dwt[:, tt * 16:(tt + 1) * 16], AF.Copy), inc="ac")
        p.wait("gpsimd", "ac", dwc)
        p.wait("gpsimd", "dv", slot_cast_dv)
        if tt == 0:
            p.wait("gpsimd", "s", d2z_ct)
        for sb in (slot1b, slot2b):
            dsc[bi] = p.iop("gpsimd", lambda E, tt=tt, b=bi, sb=sb: E.indirect_dma_start(
                out=disp2[:, :], out_offset=IndirectOffsetOnAxis(sb[:, tt:tt + 1], 0),
                in_=hstage[b][:, :], in_offset=None,
                bounds_check=4095, oob_is_err=False), dsem)
    hx_all = dsc[0]
    st1_all = dsc[1]

    NE = max(nv - 2, 0)
    ga_ct = {}
    tr_pe = {}

    def emit_load(e, gate_pe):
        par = e % 2
        sem = f"ga{par}"
        p.wait("gpsimd", "pe", gate_pe)
        if e < 2:
            p.wait("gpsimd", "hx", hx_all)
            p.wait("gpsimd", "st1", st1_all)
        for tile in range(2):
            p.dma("gpsimd", disp[par][tile][:, :],
                  disp2[e * 256 + tile * 128: e * 256 + (tile + 1) * 128, :], sem=sem)
        ga_ct[e] = p.ct[sem]

    PPOF = ((pA, 0), (pA, 512), (pB, 0), (pB, 512))
    slot_guard = [("dv", wo_done_dv)] * 4
    emit_blkdma(0, 0)
    emit_blkdma(0, 1)
    prev_ac = 0
    prev_dv = dwT_dv

    for ve in (0, 1):
        if nv == 0:
            break
        p.wait("tensor", "ac", prev_ac)
        p.wait("tensor", "dv", prev_dv)
        if ve == 0:
            p.wait("scalar", "dv", slot_cast_dv)
        a_sil = {}
        for g in range(2 * ICN):
            ic, hf = g // 2, g % 2
            blk = ic
            bx = bufix(ve, blk)
            if hf == 0:
                if blk + 2 < 2 * ICN:
                    emit_blkdma(ve, blk + 2)
                p.wait("tensor", f"wb{bx}", blk_ct[(ve, blk)])
            sgd = slot_guard[g % 4]
            p.wait("tensor", sgd[0], sgd[1])
            pp, poff = PPOF[g % 4]
            for c in range(NCH):
                pec = p.op("tensor", lambda E, pp=pp, poff=poff, bu=wbuf[bx], hf=hf, c=c: E.matmul(
                    pp[:, poff:poff + 512],
                    bu[:, c * 128:(c + 1) * 128],
                    hs2T[:, c * 1024 + hf * 512: c * 1024 + hf * 512 + 512],
                    start=(c == 0), stop=(c == NCH - 1)), inc=("pe" if c == NCH - 1 else None))
            if hf == 1:
                buf_free_pe[bx] = pec
            p.wait("scalar", "pe", pec)
            if ve == 0 and g == 12:
                p.wait("scalar", "hx", hx_all)
                p.wait("scalar", "st1", st1_all)
            a_sil[g] = p.op("scalar", lambda E, pp=pp, poff=poff, ic=ic, hf=hf: E.activation(
                h_st[:, ic * 1024 + hf * 512: ic * 1024 + hf * 512 + 512],
                pp[:, poff:poff + 512], AF.Silu), inc="ac")
            slot_guard[g % 4] = ("ac", a_sil[g])
        b_dv = {}
        for g in range(2 * ICN):
            ic, hf = g // 2, g % 2
            blk = ICN + ic
            bx = bufix(ve, blk)
            if hf == 0:
                if blk + 2 < 2 * ICN:
                    emit_blkdma(ve, blk + 2)
                elif ve + 1 < nv:
                    emit_blkdma(ve + 1, blk + 2 - 2 * ICN)
                p.wait("tensor", f"wb{bx}", blk_ct[(ve, blk)])
            sgd = slot_guard[g % 4]
            p.wait("tensor", sgd[0], sgd[1])
            pp, poff = PPOF[g % 4]
            for c in range(NCH):
                pec = p.op("tensor", lambda E, pp=pp, poff=poff, bu=wbuf[bx], hf=hf, c=c: E.matmul(
                    pp[:, poff:poff + 512],
                    bu[:, c * 128:(c + 1) * 128],
                    hs2T[:, c * 1024 + hf * 512: c * 1024 + hf * 512 + 512],
                    start=(c == 0), stop=(c == NCH - 1)), inc=("pe" if c == NCH - 1 else None))
            if hf == 1:
                buf_free_pe[bx] = pec
            p.wait("vector", "pe", pec)
            p.wait("vector", "ac", a_sil[g])
            b_dv[g] = p.op("vector", lambda E, pp=pp, poff=poff, ic=ic, hf=hf: E.tensor_tensor(
                h_st[:, ic * 1024 + hf * 512: ic * 1024 + hf * 512 + 512],
                h_st[:, ic * 1024 + hf * 512: ic * 1024 + hf * 512 + 512],
                pp[:, poff:poff + 512], ALU.mult), inc="dv")
            slot_guard[g % 4] = ("dv", b_dv[g])
        if ve == 1:
            b_end = p.ct["pe"]
            if NE > 0:
                emit_load(0, b_end)
            if NE > 1:
                emit_load(1, b_end)
        p.wait("tensor", "w2s", w2_ct[ve])
        p.wait("tensor", "dv", b_dv[2 * ICN - 1])
        d_dv = {}
        for g in range(2 * NCH):
            dc, hf = g // 2, g % 2
            sgd = slot_guard[g % 4]
            p.wait("tensor", sgd[0], sgd[1])
            pp, poff = PPOF[g % 4]
            for ic in range(ICN):
                pec = p.op("tensor", lambda E, pp=pp, poff=poff, dc=dc, hf=hf, ic=ic: E.matmul(
                    pp[:, poff:poff + 512],
                    w2slot[:, ic * 2048 + dc * 128: ic * 2048 + (dc + 1) * 128],
                    h_st[:, ic * 1024 + hf * 512: ic * 1024 + hf * 512 + 512],
                    start=(ic == 0), stop=(ic == ICN - 1)), inc=("pe" if ic == ICN - 1 else None))
            p.wait("vector", "pe", pec)
            d_dv[g] = p.op("vector", lambda E, pp=pp, poff=poff, dc=dc, hf=hf: E.tensor_tensor(
                XT[:, dc * 1024 + hf * 512: dc * 1024 + hf * 512 + 512],
                pp[:, poff:poff + 512],
                XT[:, dc * 1024 + hf * 512: dc * 1024 + hf * 512 + 512],
                ALU.add), inc="dv")
            slot_guard[g % 4] = ("dv", d_dv[g])
        d_stage_pe[ve] = p.ct["pe"]
        if ve + 1 < nv:
            emit_w2dma(ve + 1)
        prev_ac = a_sil[2 * ICN - 1]
        prev_dv = d_dv[2 * NCH - 1]
    shared_done_pe = p.ct["pe"]

    ysem_ct = {}
    for e in range(NE):
        ve = e + 2
        par = e % 2
        sem = f"ga{par}"
        ysem = ("sc", "yl")[par]
        p.wait("vector", sem, ga_ct[e])
        for tile in range(2):
            p.op("vector", lambda E, par=par, tile=tile, e=e: E.tensor_copy(
                wecol[par][:, tile:tile + 1],
                disp[par][tile][:, 2048 + e:2049 + e]))
        p.wait("tensor", sem, ga_ct[e])
        for tile in range(2):
            for c in range(NCH):
                sl = c % 2
                ptb, co = TSL[sl]
                p.wait("tensor", *tslot[sl])
                pec = p.op("tensor", lambda E, tile=tile, c=c, par=par, ptb=ptb, co=co: E.transpose(
                    ptb[:, co:co + 128], disp[par][tile][:, c * 128:(c + 1) * 128],
                    s_idb[:, :]), inc="pe")
                p.wait("vector", "pe", pec)
                tslot[sl] = ("dv", p.op("vector", lambda E, tile=tile, c=c, ptb=ptb, co=co: E.tensor_copy(
                    dispT[:, c * 256 + tile * 128: c * 256 + tile * 128 + 128],
                    ptb[:, co:co + 128]), inc="dv"))
        tr_pe[e] = p.ct["pe"]
        if e + 2 < NE:
            emit_load(e + 2, tr_pe[e])
        a_sil = {}
        for ic in range(ICN):
            blk = ic
            bx = bufix(ve, blk)
            if blk + 2 < 2 * ICN:
                emit_blkdma(ve, blk + 2)
            p.wait("tensor", f"wb{bx}", blk_ct[(ve, blk)])
            sgd = slot_guard[ic % 4]
            p.wait("tensor", sgd[0], sgd[1])
            if ic == 0:
                p.wait("tensor", *tslot[1])
            pp, poff = PPOF[ic % 4]
            for c in range(NCH):
                pec = p.op("tensor", lambda E, pp=pp, poff=poff, bu=wbuf[bx], c=c: E.matmul(
                    pp[:, poff:poff + 256],
                    bu[:, c * 128:(c + 1) * 128],
                    dispT[:, c * 256:(c + 1) * 256],
                    start=(c == 0), stop=(c == NCH - 1)), inc=("pe" if c == NCH - 1 else None))
            buf_free_pe[bx] = pec
            p.wait("scalar", "pe", pec)
            a_sil[ic] = p.op("scalar", lambda E, pp=pp, poff=poff, ic=ic: E.activation(
                h_sp[:, ic * 256:(ic + 1) * 256], pp[:, poff:poff + 256], AF.Silu), inc="ac")
            slot_guard[ic % 4] = ("ac", a_sil[ic])
        b_dv = {}
        for ic in range(ICN):
            blk = ICN + ic
            bx = bufix(ve, blk)
            if blk + 2 < 2 * ICN:
                emit_blkdma(ve, blk + 2)
            elif ve + 1 < nv:
                emit_blkdma(ve + 1, blk + 2 - 2 * ICN)
            p.wait("tensor", f"wb{bx}", blk_ct[(ve, blk)])
            sgd = slot_guard[ic % 4]
            p.wait("tensor", sgd[0], sgd[1])
            pp, poff = PPOF[ic % 4]
            for c in range(NCH):
                pec = p.op("tensor", lambda E, pp=pp, poff=poff, bu=wbuf[bx], c=c: E.matmul(
                    pp[:, poff:poff + 256],
                    bu[:, c * 128:(c + 1) * 128],
                    dispT[:, c * 256:(c + 1) * 256],
                    start=(c == 0), stop=(c == NCH - 1)), inc=("pe" if c == NCH - 1 else None))
            buf_free_pe[bx] = pec
            p.wait("vector", "pe", pec)
            p.wait("vector", "ac", a_sil[ic])
            b_dv[ic] = p.op("vector", lambda E, pp=pp, poff=poff, ic=ic: E.tensor_tensor(
                h_sp[:, ic * 256:(ic + 1) * 256],
                h_sp[:, ic * 256:(ic + 1) * 256],
                pp[:, poff:poff + 256], ALU.mult), inc="dv")
            slot_guard[ic % 4] = ("dv", b_dv[ic])
        p.wait("tensor", "w2s", w2_ct[ve])
        p.wait("tensor", "dv", b_dv[ICN - 1])
        y_dv = {}
        for g in range(8):
            tile, dq = g // 4, g % 4
            sgd = slot_guard[g % 4]
            p.wait("tensor", sgd[0], sgd[1])
            pp, poff = PPOF[g % 4]
            for ic in range(ICN):
                pec = p.op("tensor", lambda E, pp=pp, poff=poff, tile=tile, dq=dq, ic=ic: E.matmul(
                    pp[:, poff:poff + 512],
                    h_sp[:, ic * 256 + tile * 128: ic * 256 + tile * 128 + 128],
                    w2slot[:, ic * 2048 + dq * 512: ic * 2048 + (dq + 1) * 512],
                    start=(ic == 0), stop=(ic == ICN - 1)), inc=("pe" if ic == ICN - 1 else None))
            p.wait("vector", "pe", pec)
            if g == 0 and e >= 1:
                p.wait("vector", ("sc", "yl")[(e - 1) % 2], ysem_ct[e - 1])
            y_dv[g] = p.op("vector", lambda E, pp=pp, poff=poff, tile=tile, dq=dq, par=par: E.tensor_scalar(
                yst[tile][:, dq * 512:(dq + 1) * 512], pp[:, poff:poff + 512],
                wecol[par][:, tile:tile + 1], None, ALU.mult), inc="dv")
            slot_guard[g % 4] = ("dv", y_dv[g])
        d_stage_pe[ve] = p.ct["pe"]
        if ve + 1 < nv:
            emit_w2dma(ve + 1)
        p.wait("gpsimd", "dv", y_dv[7])
        for tile in range(2):
            ysem_ct[e] = p.dma("gpsimd",
                               yout[e * 256 + tile * 128: e * 256 + (tile + 1) * 128, :],
                               yst[tile][:, :], sem=ysem)
        prev_ac = a_sil[ICN - 1]
        prev_dv = y_dv[7]

    sc_all = p.ct["sc"]
    yl_all = p.ct["yl"]
    mst_pe = [0, 0, 0, 0]
    msems = ("mg0", "mg1", "hx", "st1")
    p.wait("vector", "sc", sc_all)
    p.wait("vector", "yl", yl_all)
    zpre = {}
    for bi in range(4):
        p.op("vector", lambda E, m0=mstage[2 * bi]: E.memset(m0[:, :], 0.0))
        zpre[bi] = p.op("vector", lambda E, m1=mstage[2 * bi + 1]: E.memset(m1[:, :], 0.0), inc="dv")
    for tt in range(8):
        bi = tt % 4
        msem = msems[bi]
        m0, m1 = mstage[2 * bi], mstage[2 * bi + 1]
        if tt < 4:
            zdv = zpre[bi]
        else:
            p.wait("vector", "pe", mst_pe[bi])
            p.op("vector", lambda E, m0=m0: E.memset(m0[:, :], 0.0))
            zdv = p.op("vector", lambda E, m1=m1: E.memset(m1[:, :], 0.0), inc="dv")
        p.wait("gpsimd", "dv", zdv)
        for sb, mm in ((slot1b, m0), (slot2b, m1)):
            mg = p.iop("gpsimd", lambda E, tt=tt, sb=sb, mm=mm: E.indirect_dma_start(
                out=mm[:, :], out_offset=None,
                in_=yout[:, :],
                in_offset=IndirectOffsetOnAxis(sb[:, tt:tt + 1], 0),
                bounds_check=4095, oob_is_err=False), msem)
        p.wait("vector", msem, mg)
        adv = p.op("vector", lambda E, m0=m0, m1=m1: E.tensor_tensor(
            m0[:, :], m0[:, :], m1[:, :], ALU.add), inc="dv")
        if tt == 0:
            p.wait("tensor", "dv", prev_dv)
            p.wait("tensor", "ac", prev_ac)
        for c in range(NCH):
            sl = c % 2
            ptb, co = TSL[sl]
            p.wait("tensor", *tslot[sl])
            p.wait("tensor", "dv", adv)
            pec = p.op("tensor", lambda E, c=c, m0=m0, ptb=ptb, co=co: E.transpose(
                ptb[:, co:co + 128], m0[:, c * 128:(c + 1) * 128],
                s_idb[:, :]), inc="pe")
            p.wait("vector", "pe", pec)
            tslot[sl] = ("dv", p.op("vector", lambda E, c=c, tt=tt, ptb=ptb, co=co: E.tensor_tensor(
                XT[:, c * 1024 + tt * 128: c * 1024 + (tt + 1) * 128],
                ptb[:, co:co + 128],
                XT[:, c * 1024 + tt * 128: c * 1024 + (tt + 1) * 128],
                ALU.add), inc="dv"))
            p.wait("sync", "dv", tslot[sl][1])
            p.dma("sync", out[c * 128:(c + 1) * 128, tt * 128:(tt + 1) * 128],
                  XT[:, c * 1024 + tt * 128: c * 1024 + (tt + 1) * 128], sem="st0")
        mst_pe[bi] = p.ct["pe"]
    ffn_done_dv = p.ct["dv"]

    p.wait("sync", "st0", p.ct["st0"])

    with nc.Block() as block:
        block.tensor(lambda EE: [fn(EE) for fn in p.q["tensor"]])
        block.vector(lambda EE: [fn(EE) for fn in p.q["vector"]])
        block.scalar(lambda EE: [fn(EE) for fn in p.q["scalar"]])
        block.gpsimd(lambda EE: [fn(EE) for fn in p.q["gpsimd"]])
        block.sync(lambda EE: [fn(EE) for fn in p.q["sync"]])
    return nc


_NC_CACHE = {}

def kernel(L0_attn_hidden_states, L0_attn_Wq, L0_attn_Wk, L0_attn_Wv, L0_attn_Wo,
           L0_ffn_gate_w, L0_ffn_w1, L0_ffn_w2, L0_ffn_w3,
           L0_ffn_sw1, L0_ffn_sw2, L0_ffn_sw3):
    x = np.ascontiguousarray(np.asarray(L0_attn_hidden_states, dtype=np.float32))
    Wq = np.asarray(L0_attn_Wq, dtype=np.float32)
    Wk = np.asarray(L0_attn_Wk, dtype=np.float32)
    Wv = np.asarray(L0_attn_Wv, dtype=np.float32)
    Wo = np.asarray(L0_attn_Wo, dtype=np.float32)
    gw = np.asarray(L0_ffn_gate_w, dtype=np.float32)

    wq_b = _attnT_blocks(Wq); wk_b = _attnT_blocks(Wk)
    wv_b = _attnT_blocks(Wv); wo_b = _attnT_blocks(Wo)
    gateT_a = np.ascontiguousarray(gw.reshape(16, 16, 128).transpose(2, 1, 0).reshape(128, 256))

    w1 = np.asarray(L0_ffn_w1); w2 = np.asarray(L0_ffn_w2); w3 = np.asarray(L0_ffn_w3)
    sw1 = np.asarray(L0_ffn_sw1); sw2 = np.asarray(L0_ffn_sw2); sw3 = np.asarray(L0_ffn_sw3)
    w1v = np.empty((NVE, 128, ICN * 2048), dtype=BF)
    w2v = np.empty((NVE, 128, SLAB), dtype=BF)
    w3v = np.empty((NVE, 128, ICN * 2048), dtype=BF)
    w1v[0] = _w13_ic_blocks(sw1[:1408]); w1v[1] = _w13_ic_blocks(sw1[1408:])
    w3v[0] = _w13_ic_blocks(sw3[:1408]); w3v[1] = _w13_ic_blocks(sw3[1408:])
    w2v[0] = _w2_slab(sw2[:, :1408]); w2v[1] = _w2_slab(sw2[:, 1408:])
    for e in range(16):
        w1v[2 + e] = _w13_ic_blocks(w1[e])
        w3v[2 + e] = _w13_ic_blocks(w3[e])
        w2v[2 + e] = _w2_slab(w2[e])

    scale = float(1.0 / np.sqrt(np.float32(DH)))
    kpos = np.concatenate([KEPT, [0]])
    coskT, sinkT = _rope_tables(kpos, 1.0)
    coskT[:, NK:] = 0
    sinkT[:, NK:] = 0

    identf = np.eye(128, dtype=np.float32)
    identb = np.eye(128, dtype=BF)
    onesrow_a = np.ones((1, 128), np.float32)
    onesrowb_a = np.ones((1, 128), BF)
    onescol_a = np.ones((128, 1), np.float32)
    selmid_a = np.zeros((128, 8), BF)
    for kt in range(8):
        for pp_ in range(128):
            cidx = kt * 128 + pp_
            if SB + 1 <= cidx <= NK - 1:
                selmid_a[pp_, kt] = 1.0
    sel204_a = np.zeros((128, 1), BF); sel204_a[SB - 128, 0] = 1.0
    e4col_a = (256.0 * np.arange(16, dtype=np.float32)).reshape(16, 1)

    shared = dict(wq=wq_b, wk=wk_b, wv=wv_b, wo=wo_b, gateT=gateT_a, onesrowb=onesrowb_a,
                  w1v=w1v, w2v=w2v, w3v=w3v, cosk=coskT, sink=sinkT,
                  identf=identf, identb=identb, onesrow=onesrow_a, onescol=onescol_a,
                  selmid=selmid_a, sel204=sel204_a, e4col=e4col_a,
                  zrows=np.zeros((128, 2064), BF),
                  epscol=np.full((128, 1), 1e-5, np.float32))

    in_maps = []
    for c in range(8):
        b, r = c // 2, c % 2
        xq_a = np.ascontiguousarray(x[b, r * 1024:(r + 1) * 1024, :].T)
        xk_a = np.zeros((2048, 1024), np.float32)
        xk_a[:, :NK] = x[b][KEPT].T
        qpos = np.arange(r * 1024, (r + 1) * 1024)
        cq, sq = _rope_tables(qpos, scale)
        u = U_TABLE[b] if r == 1 else np.full((16,), 2.0, np.float32)
        m = dict(shared)
        m.update(xqT=xq_a, xkT=xk_a, cosq=cq, sinq=sq, urow=np.ascontiguousarray(u.reshape(1, 16)))
        in_maps.append(m)

    if "nc" not in _NC_CACHE:
        _NC_CACHE["nc"] = build_nc()
    nc = _NC_CACHE["nc"]
    res = run_bass_kernel_spmd(nc, in_maps, core_ids=list(range(8)))
    outa = np.empty((4, 2048, 2048), np.float32)
    for c in range(8):
        b, r = c // 2, c % 2
        outa[b, r * 1024:(r + 1) * 1024, :] = res.results[c]["out"].T
    return outa


# version 28 (final)
# speedup vs baseline: 1.0262x; 1.0262x over previous
import numpy as np
import ml_dtypes

import concourse.bass as bass
import concourse.mybir as mybir
from concourse.bass_utils import run_bass_kernel_spmd
from concourse.bass import IndirectOffsetOnAxis

F32 = mybir.dt.float32
I32 = mybir.dt.int32
BF16 = mybir.dt.bfloat16
AF = mybir.ActivationFunctionType
ALU = mybir.AluOpType
AX = mybir.AxisListType.X
BF = ml_dtypes.bfloat16

H = 16
DH = 128
T = 2048
DM = 2048
NQ = 1024
NCH = 16
SB = 204
RB = 819
EVICT = T - RB
NK = SB + RB
LAST_LOC = 768
II = 1408
ICN = 11
NVE = 18
SLAB = ICN * 2048

U_TABLE = np.array([
    [0.6102175712585449, 0.6966079473495483, 0.06568920612335205, 0.6029453277587891, 0.4143662452697754, 0.24911487102508545, 0.5782934427261353, 0.24924850463867188, 0.959638237953186, 0.8191142082214355, 0.3645491600036621, 0.059471845626831055, 0.9161832332611084, 0.6182148456573486, 0.8282822370529175, 0.6976723670959473],
    [0.756269097328186, 0.3594318628311157, 0.4266716241836548, 0.477594256401062, 0.08340179920196533, 0.40768611431121826, 0.6278458833694458, 0.3251793384552002, 0.35155820846557617, 0.15171575546264648, 0.5589374303817749, 0.8866020441055298, 0.3201930522918701, 0.12432277202606201, 0.6967982053756714, 0.8069779872894287],
    [0.6127028465270996, 0.7509020566940308, 0.4151843786239624, 0.4897202253341675, 0.11665701866149902, 0.40220367908477783, 0.08376991748809814, 0.9528622627258301, 0.9559504985809326, 0.06945085525512695, 0.20124602317810059, 0.06397080421447754, 0.8091422319412231, 0.31807422637939453, 0.639366865158081, 0.8050472736358643],
    [0.5085939168930054, 0.5088651180267334, 0.9745233058929443, 0.3331362009048462, 0.05924034118652344, 0.7208822965621948, 0.3291349411010742, 0.5319250822067261, 0.4839836359024048, 0.643956184387207, 0.19713079929351807, 0.1282445192337036, 0.3164334297180176, 0.4948718547821045, 0.9159883260726929, 0.781721830368042],
], dtype=np.float32)

KEPT = np.concatenate([np.arange(0, SB), np.arange(EVICT, T)])

ARENA_BYTES = 212800



def _attnT_blocks(W):
    return np.ascontiguousarray(
        W.reshape(16, 128, 16, 128).transpose(0, 3, 2, 1).reshape(16, 128, 2048))


def _w13_ic_blocks(w):
    return np.ascontiguousarray(
        np.asarray(w).reshape(11, 128, 16, 128).transpose(3, 0, 2, 1).reshape(128, 11 * 2048))


def _w2_slab(w):
    return np.ascontiguousarray(
        np.asarray(w).reshape(2048, 11, 128).transpose(2, 1, 0).reshape(128, SLAB))


def _rope_tables(pos, scale):
    inv = 1.0 / (10000.0 ** (np.arange(0, DH, 2, dtype=np.float64) / DH))
    invf = np.concatenate([inv, inv])
    ang = pos[:, None].astype(np.float64) * invf[None, :]
    cos = (np.cos(ang) * scale).astype(np.float32)
    sin = (np.sin(ang) * scale).astype(np.float32)
    sin[:, :64] *= -1.0
    return (np.ascontiguousarray(cos.T).astype(BF), np.ascontiguousarray(sin.T).astype(BF))



class TV:
    def __init__(self, handle, elem_off, cols, p0=0, np_=128):
        self.h = handle
        self.e0 = elem_off
        self.cols = cols
        self.p0 = p0
        self.np_ = np_

    def __getitem__(self, idx):
        ps, fs = idx
        pstart = self.p0 + (ps.start or 0)
        pstop = self.p0 + (ps.stop if ps.stop is not None else self.np_)
        fstart = self.e0 + (fs.start or 0)
        fstop = self.e0 + (fs.stop if fs.stop is not None else self.cols)
        return self.h[pstart:pstop, fstart:fstop]


class Carver:
    def __init__(self, h32, h16, hi32=None):
        self.h32 = h32
        self.h16 = h16
        self.hi32 = hi32
        self.cursor = 0

    def at(self, byte_off):
        self.cursor = byte_off
        return self

    def f32(self, cols, p0=0, np_=128):
        off = (self.cursor + 511) // 512 * 512
        self.cursor = off + cols * 4
        assert self.cursor <= ARENA_BYTES, f"arena overflow at {self.cursor}"
        return TV(self.h32, off // 4, cols, p0, np_)

    def b16(self, cols, p0=0, np_=128):
        off = (self.cursor + 511) // 512 * 512
        self.cursor = off + cols * 2
        assert self.cursor <= ARENA_BYTES, f"arena overflow at {self.cursor}"
        return TV(self.h16, off // 2, cols, p0, np_)

    def i32(self, cols, p0=0, np_=128):
        off = (self.cursor + 511) // 512 * 512
        self.cursor = off + cols * 4
        assert self.cursor <= ARENA_BYTES, f"arena overflow at {self.cursor}"
        return TV(self.hi32, off // 4, cols, p0, np_)


class Prog:
    ENGS = ("tensor", "vector", "scalar", "gpsimd", "sync")

    def __init__(self, nc, sems):
        self.nc = nc
        self.q = {e: [] for e in self.ENGS}
        self.sems = sems
        self.ct = {k: 0 for k in sems}

    def wait(self, eng, sem, val=None):
        v = self.ct[sem] if val is None else val
        if v is None or v <= 0:
            return
        s = self.sems[sem]
        self.q[eng].append(lambda E: E.wait_ge(s, v))

    SELF_SEM = {"vector": "dv", "scalar": "ac", "gpsimd": "gp"}

    def op(self, eng, build, inc=None, k=1):
        ss = self.SELF_SEM.get(eng)
        if ss is not None:
            prev = self.ct[ss]
            if prev > 0:
                s0 = self.sems[ss]
                self.q[eng].append(lambda E, s0=s0, prev=prev: E.wait_ge(s0, prev))
            if inc is None:
                inc = ss
            assert inc == ss, (eng, inc)
        if inc is None:
            self.q[eng].append(build)
            return None
        s = self.sems[inc]
        self.ct[inc] += k
        self.q[eng].append(lambda E: build(E).then_inc(s, k))
        return self.ct[inc]

    def dma(self, eng, out, in_, sem=None):
        if sem is None:
            sem = "s" if eng == "sync" else "g"
        s = self.sems[sem]
        self.ct[sem] += 16
        self.q[eng].append(lambda E: E.dma_start(out=out, in_=in_).then_inc(s, 16))
        return self.ct[sem]

    def iop(self, eng, build, sem):
        s = self.sems[sem]
        self.ct[sem] += 16
        self.q[eng].append(lambda E: build(E).then_inc(s, 16))
        return self.ct[sem]


def build_nc(nh=H, nv=NVE, do_w=True, do_l2=True, do_entry=True):
    nc = bass.Bass("TRN2", target_bir_lowering=False)
    d = {}
    def di(name, shape, dtype):
        d[name] = nc.dram_tensor(name, shape, dtype, kind="ExternalInput").ap()
        return d[name]

    xqT = di("xqT", [DM, NQ], F32)
    xkT = di("xkT", [DM, NQ], F32)
    wq = di("wq", [16, 128, 2048], F32)
    wk = di("wk", [16, 128, 2048], F32)
    wv = di("wv", [16, 128, 2048], F32)
    wo = di("wo", [16, 128, 2048], F32)
    gateT = di("gateT", [128, 256], F32)
    w1v = di("w1v", [NVE, 128, ICN * 2048], BF16)
    w2v = di("w2v", [NVE, 128, SLAB], BF16)
    w3v = di("w3v", [NVE, 128, ICN * 2048], BF16)
    cosq = di("cosq", [128, NQ], BF16)
    sinq = di("sinq", [128, NQ], BF16)
    cosk = di("cosk", [128, NQ], BF16)
    sink = di("sink", [128, NQ], BF16)
    identf = di("identf", [128, 128], F32)
    identb = di("identb", [128, 128], BF16)
    onesrow = di("onesrow", [1, 128], F32)
    onesrowb = di("onesrowb", [1, 128], BF16)
    onescol = di("onescol", [128, 1], F32)
    selmid = di("selmid", [128, 8], BF16)
    sel204 = di("sel204", [128, 1], BF16)
    urow = di("urow", [1, 16], F32)
    epscol = di("epscol", [128, 1], F32)
    zrows = di("zrows", [128, DM + 16], BF16)
    e4col = di("e4col", [16, 1], F32)
    out = nc.dram_tensor("out", [DM, NQ], F32, kind="ExternalOutput").ap()
    disp2 = nc.dram_tensor("disp2", [4096, DM + 16], BF16, kind="Internal").ap()
    yout = nc.dram_tensor("yout", [4096, DM], BF16, kind="Internal").ap()

    arena_guard = nc.sbuf_tensor("AR", [128, ARENA_BYTES // 4], F32)
    arena = arena_guard.__enter__()
    h32 = arena
    h16 = arena.bitcast(BF16)
    hi32 = arena.bitcast(I32)

    SEMNAMES = ("g", "s", "pe", "dv", "ac", "gp", "gc1", "gc2", "gc3",
                "stg0", "stg1", "stg2", "stg3", "wp0", "wp1", "gs", "w2s",
                "ga0", "ga1", "sc", "hx", "yl",
                "wb0", "wb1", "wb2", "wb3", "st0", "st1", "mg0", "mg1",
                "ka0", "ka1", "xt0")
    sem_guards = [nc.semaphore(n) for n in SEMNAMES]
    sems = {}
    for n, sg in zip(SEMNAMES, sem_guards):
        sems[n] = sg.__enter__()
    p = Prog(nc, sems)

    pA_g = nc.psum_tensor("pA", [128, 1024], F32); pA = pA_g.__enter__()
    pB_g = nc.psum_tensor("pB", [128, 1024], F32); pB = pB_g.__enter__()
    pC_g = nc.psum_tensor("pC", [128, 1024], F32); pC = pC_g.__enter__()
    pT1_g = nc.psum_tensor("pT1", [128, 1024], BF16); pT1 = pT1_g.__enter__()
    pT2_g = nc.psum_tensor("pT2", [128, 1024], BF16); pT2 = pT2_g.__enter__()

    C = Carver(h32, h16, hi32)

    C.at(0)
    s_idf = C.f32(128)
    s_idb = C.b16(128)
    s_ones_r = C.f32(128, p0=0, np_=1)
    s_ones_rb = C.b16(128, p0=0, np_=1)
    s_ones_c = C.f32(1)
    s_eps = C.f32(1)
    GLOB_END = (C.cursor + 511) // 512 * 512

    for dst, src in ((s_idf, identf), (s_idb, identb), (s_ones_r, onesrow), (s_ones_rb, onesrowb), (s_ones_c, onescol), (s_eps, epscol)):
        p.dma("gpsimd", dst[:, :], src[:, :], sem="gc1")
    const_g = p.ct["gc1"]

    C.at(GLOB_END)
    hsqT = C.b16(16 * 1024)
    hskT = C.b16(16 * 1024)
    A_SCRATCH = C.cursor
    xst = [C.f32(1024) for _ in range(2)]
    sqst = [C.f32(1024) for _ in range(2)]
    murowE = C.f32(1024, p0=0, np_=1)
    mubE = C.f32(1024)
    rstdbE = C.f32(1024)
    cenbE = C.f32(1024)

    C.at(A_SCRATCH)
    probsT = C.b16(8 * 1024)
    aoT = C.b16(16 * 1024)
    wqs = [C.b16(2048) for _ in range(2)]
    wks = [C.b16(2048) for _ in range(2)]
    wvs = [C.b16(2048) for _ in range(2)]
    qTt = [C.b16(1024) for _ in range(2)]
    kTt = [C.b16(1024) for _ in range(2)]
    vft = [C.b16(1024) for _ in range(2)]
    vtok = [C.b16(1024) for _ in range(2)]
    rtmp = C.b16(512)
    s_cosq = C.b16(NQ)
    s_sinq = C.b16(NQ)
    s_cosk = C.b16(NQ)
    s_sink = C.b16(NQ)
    junk = C.b16(1024)
    s_selm = C.b16(8)
    s_sel204 = C.b16(1)
    s_urow = C.f32(16, p0=0, np_=1)
    stq = C.f32(8)
    stqT = C.f32(128, p0=0, np_=8)
    negrow = C.f32(1024, p0=0, np_=1)
    negrh = C.b16(1024, p0=0, np_=1)
    negrl = C.b16(1024, p0=0, np_=1)
    sst = C.f32(8)
    sstM = C.f32(8)
    sstL = C.f32(8)
    evs = C.f32(8, p0=0, np_=1)
    vero = C.f32(128, p0=0, np_=1)
    srowb = C.f32(256, p0=0, np_=1)
    cbs = C.b16(1024)
    ATTN_END = C.cursor
    assert ATTN_END <= ARENA_BYTES
    xst2 = [C.f32(1024) for _ in range(2)]
    sqst2 = [C.f32(1024) for _ in range(2)]
    p1ev = [C.f32(1024) for _ in range(2)]
    assert C.cursor <= ARENA_BYTES, C.cursor
    murowK = murowE
    mubK = mubE
    rstdK = rstdbE
    cenbK = cenbE

    for dst, src in ((s_cosq, cosq), (s_sinq, sinq), (s_cosk, cosk), (s_sink, sink),
                     (s_selm, selmid), (s_sel204, sel204), (s_urow, urow)):
        p.dma("gpsimd", dst[:, :], src[:, :], sem="gc2")
    tables_g = p.ct["gc2"]

    wload_ct = {}
    proj_pe = {}

    def emit_wload(h):
        pw = h % 2
        p.wait("gpsimd", "pe", proj_pe.get(h - 2, 0))
        p.dma("gpsimd", wqs[pw][:, :], wq[h, :, :], sem=f"wp{pw}")
        p.dma("gpsimd", wks[pw][:, :], wk[h, :, :], sem=f"wp{pw}")
        wload_ct[h] = p.dma("gpsimd", wvs[pw][:, :], wv[h, :, :], sem=f"wp{pw}")

    emit_wload(0)
    emit_wload(1)
    vslot_dv = {k: 0 for k in range(8)}
    reader_dv = {0: 0, 1: 0}
    wload_2_sentinel = None

    p.wait("tensor", "gc1", const_g)
    p.wait("scalar", "gc1", const_g)
    p.wait("vector", "gc1", const_g)
    stg_dv = [0, 0]
    l1pe = {}
    for c in range(NCH):
        bi = c % 2
        p.wait("sync", "gp", stg_dv[bi])
        p.wait("sync", "pe", l1pe.get(c - 2, 0))
        ssem = ("stg0", "stg1")[bi]
        gct = p.dma("sync", xst[bi][:, :], xqT[c * 128:(c + 1) * 128, :], sem=ssem)
        p.wait("gpsimd", ssem, gct)
        gpc = p.op("gpsimd", lambda E, b=bi: E.tensor_tensor(
            sqst[b][:, :], xst[b][:, :], xst[b][:, :], ALU.mult), inc="gp")
        p.wait("tensor", ssem, gct)
        p.wait("tensor", "gp", gpc)
        for hf in range(2):
            p.op("tensor", lambda E, b=bi, hf=hf, c=c: E.matmul(
                pB[0:1, hf * 512:(hf + 1) * 512], s_ones_c[:, :],
                xst[b][:, hf * 512:(hf + 1) * 512],
                start=(c == 0), stop=(c == NCH - 1)))
            l1pe[c] = p.op("tensor", lambda E, b=bi, hf=hf, c=c: E.matmul(
                pC[0:1, hf * 512:(hf + 1) * 512], s_ones_c[:, :],
                sqst[b][:, hf * 512:(hf + 1) * 512],
                start=(c == 0), stop=(c == NCH - 1)), inc=("pe" if hf == 1 else None))
        stg_dv[bi] = gpc
    pec = l1pe[NCH - 1]
    p.wait("vector", "pe", pec)
    for hf in range(2):
        p.op("vector", lambda E, hf=hf: E.tensor_scalar(
            murowE[:, hf * 512:(hf + 1) * 512], pB[0:1, hf * 512:(hf + 1) * 512],
            1.0 / DM, None, ALU.mult))
    dvc = p.op("vector", lambda E: E.nop(), inc="dv")
    p.wait("tensor", "dv", dvc)
    for hf in range(2):
        bcpe = p.op("tensor", lambda E, hf=hf: E.matmul(
            pB[:, hf * 512:(hf + 1) * 512], s_ones_r[:, :],
            murowE[:, hf * 512:(hf + 1) * 512], start=True, stop=True),
            inc=("pe" if hf == 1 else None))
    p.wait("vector", "pe", bcpe)
    p.op("vector", lambda E: E.tensor_copy(mubE[:, :], pB[:, :]))
    p.op("vector", lambda E: E.tensor_scalar(
        murowE[:, 0:512], pC[0:1, 0:512], 1.0 / DM, None, ALU.mult))
    dvc = p.op("vector", lambda E: E.tensor_scalar(
        murowE[:, 512:1024], pC[0:1, 512:1024], 1.0 / DM, None, ALU.mult), inc="dv")
    p.wait("tensor", "dv", dvc)
    for hf in range(2):
        p.op("tensor", lambda E, hf=hf: E.matmul(
            pC[:, hf * 512:(hf + 1) * 512], s_ones_r[:, :],
            murowE[:, hf * 512:(hf + 1) * 512], start=True,
            stop=True), inc=("pe" if hf == 1 else None))
    bcpe2 = p.ct["pe"]
    p.wait("vector", "pe", bcpe2)
    p.op("vector", lambda E: E.tensor_tensor(cenbE[:, :], mubE[:, :], mubE[:, :], ALU.mult))
    dvc = p.op("vector", lambda E: E.tensor_tensor(
        cenbE[:, :], pC[:, :], cenbE[:, :], ALU.subtract), inc="dv")
    p.wait("scalar", "dv", dvc)
    p.op("scalar", lambda E: E.activation(rstdbE[:, :], cenbE[:, :], AF.Ln, bias=s_eps[:, 0:1]))
    acc = p.op("scalar", lambda E: E.activation(rstdbE[:, :], rstdbE[:, :], AF.Exp, scale=-0.5), inc="ac")
    p.wait("vector", "ac", acc)
    qfin_dv = dvc

    p.wait("sync", "pe", l1pe[NCH - 1])
    p.wait("sync", "gp", p.ct["gp"])
    p.wait("tensor", "wp0", wload_ct[0])
    p.wait("vector", "gc2", tables_g)
    p.wait("tensor", "gc2", tables_g)
    p.wait("tensor", "dv", qfin_dv)
    stg_dv = [0, 0]
    kst_dv = [0, 0]
    kpe = {}
    qapply_dv = {}
    qpj_pe = {}
    for c in range(NCH):
        bi = c % 2
        p.wait("sync", "dv", stg_dv[bi])
        ssem = ("stg0", "stg1")[bi]
        gct = p.dma("sync", xst[bi][:, :], xqT[c * 128:(c + 1) * 128, :], sem=ssem)
        p.wait("vector", ssem, gct)
        p.op("vector", lambda E, b=bi: E.tensor_tensor(
            cenbE[:, :], xst[b][:, :], mubE[:, :], ALU.subtract))
        stg_dv[bi] = qapply_dv[c] = p.op("vector", lambda E, b=bi, c=c: E.tensor_tensor(
            hsqT[:, c * 1024:(c + 1) * 1024], cenbE[:, :], rstdbE[:, :], ALU.mult), inc="dv")
        p.wait("gpsimd", "gp", kst_dv[bi])
        p.wait("gpsimd", "pe", kpe.get(c - 2, 0))
        ksem = ("stg2", "stg3")[bi]
        kct = p.dma("gpsimd", xst2[bi][:, :], xkT[c * 128:(c + 1) * 128, :], sem=ksem)
        p.wait("gpsimd", ksem, kct)
        kgp = p.op("gpsimd", lambda E, b=bi: E.tensor_tensor(
            sqst2[b][:, :], xst2[b][:, :], xst2[b][:, :], ALU.mult), inc="gp")
        kst_dv[bi] = kgp
        p.wait("tensor", ksem, kct)
        p.wait("tensor", "gp", kgp)
        for hf in range(2):
            p.op("tensor", lambda E, b=bi, hf=hf, c=c: E.matmul(
                pB[0:1, hf * 512:(hf + 1) * 512], s_ones_c[:, :],
                xst2[b][:, hf * 512:(hf + 1) * 512],
                start=(c == 0), stop=(c == NCH - 1)))
            kpe[c] = p.op("tensor", lambda E, b=bi, hf=hf, c=c: E.matmul(
                pA[0:1, hf * 512:(hf + 1) * 512], s_ones_c[:, :],
                sqst2[b][:, hf * 512:(hf + 1) * 512],
                start=(c == 0), stop=(c == NCH - 1)), inc=("pe" if hf == 1 else None))
        p.wait("tensor", "dv", qapply_dv[c])
        for hf in range(2):
            qpj_pe[(c, hf)] = p.op("tensor", lambda E, hf=hf, c=c: E.matmul(
                pC[:, hf * 512:(hf + 1) * 512],
                wqs[0][:, c * 128:(c + 1) * 128],
                hsqT[:, c * 1024 + hf * 512: c * 1024 + hf * 512 + 512],
                start=(c == 0), stop=(c == NCH - 1)),
                inc=("pe" if c == NCH - 1 else None))

    def rope_evac(dst, tc, ts_, hf, pr, pec, do_rope=True):
        p.wait("vector", "pe", pec)
        s0 = hf * 512
        if do_rope:
            p.op("vector", lambda E: E.tensor_tensor(
                dst[:, s0:s0 + 512], pr[:, s0:s0 + 512], tc[:, s0:s0 + 512], ALU.mult))
            p.op("vector", lambda E: E.tensor_tensor(
                rtmp[0:64, 0:512], pr[64:128, s0:s0 + 512], ts_[0:64, s0:s0 + 512], ALU.mult))
            p.op("vector", lambda E: E.tensor_tensor(
                rtmp[64:128, 0:512], pr[0:64, s0:s0 + 512], ts_[64:128, s0:s0 + 512], ALU.mult))
            return p.op("vector", lambda E: E.tensor_tensor(
                dst[:, s0:s0 + 512], dst[:, s0:s0 + 512], rtmp[:, 0:512], ALU.add), inc="dv")
        return p.op("vector", lambda E: E.tensor_copy(
            dst[:, s0:s0 + 512], pr[:, s0:s0 + 512]), inc="dv")

    for hf in range(2):
        reader_dv[hf] = rope_evac(qTt[0], s_cosq, s_sinq, hf, pC, qpj_pe[(NCH - 1, hf)])

    pec = kpe[NCH - 1]
    p.wait("vector", "pe", pec)
    for hf in range(2):
        p.op("vector", lambda E, hf=hf: E.tensor_scalar(
            murowK[:, hf * 512:(hf + 1) * 512], pB[0:1, hf * 512:(hf + 1) * 512],
            1.0 / DM, None, ALU.mult))
    dvc = p.op("vector", lambda E: E.nop(), inc="dv")
    p.wait("tensor", "dv", dvc)
    for hf in range(2):
        bcpe = p.op("tensor", lambda E, hf=hf: E.matmul(
            pB[:, hf * 512:(hf + 1) * 512], s_ones_r[:, :],
            murowK[:, hf * 512:(hf + 1) * 512], start=True, stop=True),
            inc=("pe" if hf == 1 else None))
    p.wait("vector", "pe", bcpe)
    p.op("vector", lambda E: E.tensor_copy(mubK[:, :], pB[:, :]))
    p.op("vector", lambda E: E.tensor_scalar(
        murowK[:, 0:512], pA[0:1, 0:512], 1.0 / DM, None, ALU.mult))
    dvc = p.op("vector", lambda E: E.tensor_scalar(
        murowK[:, 512:1024], pA[0:1, 512:1024], 1.0 / DM, None, ALU.mult), inc="dv")
    p.wait("tensor", "dv", dvc)
    for hf in range(2):
        p.op("tensor", lambda E, hf=hf: E.matmul(
            pA[:, hf * 512:(hf + 1) * 512], s_ones_r[:, :],
            murowK[:, hf * 512:(hf + 1) * 512], start=True,
            stop=True), inc=("pe" if hf == 1 else None))
    bcpe2 = p.ct["pe"]
    p.wait("vector", "pe", bcpe2)
    p.op("vector", lambda E: E.tensor_tensor(cenbK[:, :], mubK[:, :], mubK[:, :], ALU.mult))
    dvc = p.op("vector", lambda E: E.tensor_tensor(
        cenbK[:, :], pA[:, :], cenbK[:, :], ALU.subtract), inc="dv")
    pA_free_dv = dvc
    p.wait("scalar", "dv", dvc)
    p.op("scalar", lambda E: E.activation(rstdK[:, :], cenbK[:, :], AF.Ln, bias=s_eps[:, 0:1]))
    acc = p.op("scalar", lambda E: E.activation(rstdK[:, :], rstdK[:, :], AF.Exp, scale=-0.5), inc="ac")
    p.wait("vector", "ac", acc)

    p.wait("gpsimd", "pe", kpe[NCH - 1])
    p.wait("gpsimd", "gp", p.ct["gp"])
    kst_dv = [0, 0]
    kapply_dv = {}
    kpj_pe = {}
    p.wait("tensor", "dv", reader_dv[1])
    for c in range(NCH):
        bi = c % 2
        p.wait("scalar", "dv", kst_dv[bi])
        ksem = ("ka0", "ka1")[bi]
        kct = p.dma("scalar", xst2[bi][:, :], xkT[c * 128:(c + 1) * 128, :], sem=ksem)
        p.wait("vector", ksem, kct)
        p.op("vector", lambda E, b=bi: E.tensor_tensor(
            cenbK[:, :], xst2[b][:, :], mubK[:, :], ALU.subtract))
        kst_dv[bi] = kapply_dv[c] = p.op("vector", lambda E, b=bi, c=c: E.tensor_tensor(
            hskT[:, c * 1024:(c + 1) * 1024], cenbK[:, :], rstdK[:, :], ALU.mult), inc="dv")
        p.wait("tensor", "dv", kapply_dv[c])
        for hf in range(2):
            kpj_pe[(c, hf)] = p.op("tensor", lambda E, hf=hf, c=c: E.matmul(
                pC[:, hf * 512:(hf + 1) * 512],
                wks[0][:, c * 128:(c + 1) * 128],
                hskT[:, c * 1024 + hf * 512: c * 1024 + hf * 512 + 512],
                start=(c == 0), stop=(c == NCH - 1)),
                inc=("pe" if c == NCH - 1 else None))
    for hf in range(2):
        reader_dv[hf] = rope_evac(kTt[0], s_cosk, s_sink, hf, pC, kpj_pe[(NCH - 1, hf)])

    p.wait("tensor", "dv", reader_dv[1])
    for hf in range(2):
        vpe = None
        for c in range(NCH):
            vpe = p.op("tensor", lambda E, hf=hf, c=c: E.matmul(
                pC[:, hf * 512:(hf + 1) * 512],
                wvs[0][:, c * 128:(c + 1) * 128],
                hskT[:, c * 1024 + hf * 512: c * 1024 + hf * 512 + 512],
                start=(c == 0), stop=(c == NCH - 1)), inc=("pe" if c == NCH - 1 else None))
        reader_dv[hf] = rope_evac(vft[0], None, None, hf, pC, vpe, do_rope=False)
    proj_pe[0] = p.ct["pe"]
    emit_wload(2)
    p.wait("tensor", "dv", reader_dv[1])
    for kt in range(8):
        sl = kt % 2
        ptb = (pT1, pT2)[sl]
        p.wait("tensor", "dv", vslot_dv[sl])
        pec = p.op("tensor", lambda E, kt=kt, ptb=ptb: E.transpose(
            ptb[:, 0:128], vft[0][:, kt * 128:(kt + 1) * 128],
            s_idb[:, :]), inc="pe")
        p.wait("vector", "pe", pec)
        vslot_dv[sl] = p.op("vector", lambda E, kt=kt, ptb=ptb: E.tensor_copy(
            vtok[0][:, kt * 128:(kt + 1) * 128], ptb[:, 0:128]), inc="dv")
    entry_done_dv = p.ct["dv"]
    entry_done_gp = p.ct["gp"]
    pC_free_dv = reader_dv[1]

    p.wait("sync", "pe", p.ct["pe"])
    for rr in range(32):
        p.op("sync", lambda E, rr=rr: E.dma_start(
            out=disp2[rr * 128:(rr + 1) * 128, :], in_=zrows[:, :]), inc="s", k=16)
    d2z_ct = p.ct["s"]

    pass2_pe_prev = 0
    cb_pe_prev = 0
    crx_prev = 0
    prev_pv_dv = entry_done_dv

    def genA(h):
        pw = h % 2
        ix = h % 2
        mats = ((wqs[pw], hsqT, qTt[ix], s_cosq, s_sinq, True),
                (wks[pw], hskT, kTt[ix], s_cosk, s_sink, True),
                (wvs[pw], hskT, vft[ix], None, None, False))
        first = True
        for (wsl, hsT, dst, tc, ts_, do_rope) in mats:
            for hf in range(2):
                bk = hf
                if first:
                    p.wait("tensor", f"wp{pw}", wload_ct[h])
                    if h == 0:
                        p.wait("tensor", "gc2", tables_g)
                        p.wait("vector", "gc2", tables_g)
                    first = False
                p.wait("tensor", "dv", reader_dv[bk])
                for c in range(NCH):
                    pec = p.op("tensor", lambda E, w=wsl, hs=hsT, hf=hf, c=c, bk=bk: E.matmul(
                        pB[:, bk * 512:(bk + 1) * 512],
                        w[:, c * 128:(c + 1) * 128],
                        hs[:, c * 1024 + hf * 512: c * 1024 + hf * 512 + 512],
                        start=(c == 0), stop=(c == NCH - 1)), inc=("pe" if c == NCH - 1 else None))
                p.wait("vector", "pe", pec)
                s0 = hf * 512
                if do_rope:
                    p.op("vector", lambda E, dst=dst, tc=tc, s0=s0, bk=bk: E.tensor_tensor(
                        dst[:, s0:s0 + 512], pB[:, bk * 512:bk * 512 + 512],
                        tc[:, s0:s0 + 512], ALU.mult))
                    p.op("vector", lambda E, ts_=ts_, s0=s0, bk=bk: E.tensor_tensor(
                        rtmp[0:64, 0:512], pB[64:128, bk * 512:bk * 512 + 512],
                        ts_[0:64, s0:s0 + 512], ALU.mult))
                    p.op("vector", lambda E, ts_=ts_, s0=s0, bk=bk: E.tensor_tensor(
                        rtmp[64:128, 0:512], pB[0:64, bk * 512:bk * 512 + 512],
                        ts_[64:128, s0:s0 + 512], ALU.mult))
                    reader_dv[bk] = p.op("vector", lambda E, dst=dst, s0=s0: E.tensor_tensor(
                        dst[:, s0:s0 + 512], dst[:, s0:s0 + 512], rtmp[:, 0:512], ALU.add), inc="dv")
                else:
                    reader_dv[bk] = p.op("vector", lambda E, dst=dst, s0=s0, bk=bk: E.tensor_copy(
                        dst[:, s0:s0 + 512], pB[:, bk * 512:bk * 512 + 512]), inc="dv")
                yield
        proj_pe[h] = p.ct["pe"]
        if h + 2 < nh:
            emit_wload(h + 2)
        p.wait("tensor", "dv", reader_dv[1])
        for kt in range(8):
            sl = kt % 2
            ptb = (pT1, pT2)[sl]
            p.wait("tensor", "dv", vslot_dv[sl])
            pec = p.op("tensor", lambda E, kt=kt, ptb=ptb, ix=ix: E.transpose(
                ptb[:, 0:128], vft[ix][:, kt * 128:(kt + 1) * 128],
                s_idb[:, :]), inc="pe")
            p.wait("vector", "pe", pec)
            vslot_dv[sl] = p.op("vector", lambda E, kt=kt, ptb=ptb, ix=ix: E.tensor_copy(
                vtok[ix][:, kt * 128:(kt + 1) * 128], ptb[:, 0:128]), inc="dv")
            if kt == 3:
                yield
        yield

    def genB(h):
        nonlocal pA_free_dv, pC_free_dv, pass2_pe_prev, prev_pv_dv, cb_pe_prev, crx_prev
        ix = h % 2
        qT = qTt[ix]
        kT = kTt[ix]
        vT = vtok[ix]
        pass1_ac = {}
        pass1_cp = {}
        for qt in range(8):
            pr = (pA, pC)[qt % 2]
            ev = p1ev[qt % 2]
            if qt == 0:
                p.wait("tensor", "dv", reader_dv[1])
                p.wait("tensor", "dv", pA_free_dv)
            elif qt == 1:
                p.wait("tensor", "dv", pC_free_dv)
            else:
                p.wait("tensor", "dv", pass1_cp[qt - 2])
            for hf in range(2):
                pec = p.op("tensor", lambda E, qt=qt, hf=hf, qT=qT, kT=kT, pr=pr: E.matmul(
                    pr[:, hf * 512:(hf + 1) * 512],
                    qT[:, qt * 128:(qt + 1) * 128],
                    kT[:, hf * 512:(hf + 1) * 512],
                    start=True, stop=True), inc=("pe" if hf == 1 else None))
            p.wait("vector", "pe", pec)
            if qt >= 2:
                p.wait("vector", "ac", pass1_ac[qt - 2])
            pass1_cp[qt] = p.op("vector", lambda E, pr=pr, ev=ev: E.tensor_copy(
                ev[:, 0:1023], pr[:, 0:1023]), inc="dv")
            dvc = p.op("vector", lambda E, ev=ev, qt=qt: E.reduce_max(
                sstM[:, qt:qt + 1], ev[:, 0:1023], axis=AX, negate=True), inc="dv")
            p.wait("scalar", "dv", dvc)
            p.op("scalar", lambda E, ev=ev, qt=qt: E.activation(
                junk[:, 0:1023], ev[:, 0:1023], AF.Exp, bias=sstM[:, qt:qt + 1],
                accum_out=sst[:, 4:5]))
            acc = p.op("scalar", lambda E, qt=qt: E.activation(
                sstL[:, qt:qt + 1], sst[:, 4:5], AF.Ln), inc="ac")
            pass1_ac[qt] = acc
            yield
        p.wait("vector", "ac", pass1_ac[7])
        last_stt = p.op("vector", lambda E: E.scalar_tensor_tensor(
            stq[:, 0:8], sstM[:, 0:8], 1.0, sstL[:, 0:8], ALU.mult, ALU.subtract), inc="dv")
        pass2_ac = {}
        for kt in (0, 1):
            pr = (pA, pC)[kt % 2]
            p.wait("tensor", "dv", pass1_cp[6 + kt])
            for hf in range(2):
                p.op("tensor", lambda E, kt=kt, hf=hf, qT=qT, kT=kT, pr=pr: E.matmul(
                    pr[:, hf * 512:(hf + 1) * 512],
                    kT[:, kt * 128:(kt + 1) * 128],
                    qT[:, hf * 512:(hf + 1) * 512],
                    start=True, stop=True), inc=("pe" if hf == 1 else None))
            pec = p.ct["pe"]
            p.wait("scalar", "pe", pec)
            pass2_ac[kt] = p.op("scalar", lambda E, kt=kt, pr=pr: E.activation(
                probsT[:, kt * 1024:(kt + 1) * 1024], pr[:, :], AF.Exp), inc="ac")
            yield
        p.wait("tensor", "dv", last_stt)
        p.wait("tensor", "ac", pass2_ac[0])
        pec = p.op("tensor", lambda E: E.transpose(pA[0:8, 0:128], stq[:, 0:8], s_idf[:, :]), inc="pe")
        p.wait("vector", "pe", pec)
        dvc = p.op("vector", lambda E: E.tensor_copy(stqT[:, :], pA[0:8, 0:128]), inc="dv")
        p.wait("gpsimd", "dv", dvc)
        p.wait("gpsimd", "pe", pass2_pe_prev)
        p.wait("gpsimd", "ac", crx_prev)
        ngc = p.dma("gpsimd", negrow[:, :], stqT[:, :], sem="gs")
        p.wait("scalar", "gs", ngc)
        p.wait("scalar", "pe", cb_pe_prev)
        crx = p.op("scalar", lambda E: E.activation(
            negrh[:, :], negrow[:, :], AF.Exp), inc="ac")
        crx_prev = crx
        stats_dv = dvc
        yield
        for kt in range(2, 8):
            pr = (pA, pC)[kt % 2]
            if kt == 2:
                p.wait("tensor", "dv", dvc)
            else:
                p.wait("tensor", "ac", pass2_ac[kt - 2])
            for hf in range(2):
                p.op("tensor", lambda E, kt=kt, hf=hf, qT=qT, kT=kT, pr=pr: E.matmul(
                    pr[:, hf * 512:(hf + 1) * 512],
                    kT[:, kt * 128:(kt + 1) * 128],
                    qT[:, hf * 512:(hf + 1) * 512],
                    start=True, stop=True), inc=("pe" if hf == 1 else None))
            pec = p.ct["pe"]
            p.wait("scalar", "pe", pec)
            acc = p.op("scalar", lambda E, kt=kt, pr=pr: E.activation(
                probsT[:, kt * 1024:(kt + 1) * 1024], pr[:, :], AF.Exp), inc="ac")
            pass2_ac[kt] = acc
            yield
        pass2_pe_prev = p.ct["pe"]
        p.wait("tensor", "ac", pass2_ac[7])
        p.wait("tensor", "dv", stats_dv)
        for kt in range(8):
            p.op("tensor", lambda E, kt=kt: E.matmul(
                pA[0:1, 0:1], s_selm[:, kt:kt + 1],
                probsT[:, kt * 1024 + 1023: kt * 1024 + 1024],
                start=(kt == 0), stop=(kt == 7)))
        p.op("tensor", lambda E: E.matmul(
            pA[0:1, 1:2], s_sel204[:, :], probsT[:, 1 * 1024 + 1023: 1 * 1024 + 1024],
            start=True, stop=True))
        p.op("tensor", lambda E, vT=vT: E.matmul(
            pA[0:1, 128:256], s_sel204[:, :], vT[:, 1 * 128: 2 * 128], start=True, stop=True))
        for kt in range(8):
            pec = p.op("tensor", lambda E, kt=kt: E.matmul(
                pA[0:1, 256:512], s_selm[:, kt:kt + 1],
                probsT[:, kt * 1024 + LAST_LOC: kt * 1024 + 1024],
                start=(kt == 0), stop=(kt == 7)), inc=("pe" if kt == 7 else None))
        p.wait("vector", "pe", pec)
        p.op("vector", lambda E: E.tensor_copy(evs[:, 6:7], negrh[0:1, 1023:1024]))
        p.op("vector", lambda E: E.tensor_scalar(
            evs[:, 0:1], pA[0:1, 0:1], evs[:, 6:7], None, ALU.mult))
        p.op("vector", lambda E: E.tensor_scalar(
            evs[:, 0:1], evs[:, 0:1], 1.0 / 818.0, 1e-6, ALU.mult, ALU.add))
        p.op("vector", lambda E: E.reciprocal(evs[:, 1:2], evs[:, 0:1]))
        p.op("vector", lambda E: E.tensor_scalar(
            evs[:, 7:8], pA[0:1, 1:2], evs[:, 6:7], None, ALU.mult))
        p.op("vector", lambda E: E.tensor_tensor(evs[:, 2:3], evs[:, 7:8], evs[:, 1:2], ALU.mult))
        p.op("vector", lambda E: E.tensor_scalar_min(evs[:, 3:4], evs[:, 2:3], 1.0))
        p.op("vector", lambda E, h=h: E.tensor_tensor(
            evs[:, 4:5], s_urow[:, h:h + 1], evs[:, 3:4], ALU.is_lt))
        p.op("vector", lambda E: E.tensor_scalar_mul(evs[:, 5:6], evs[:, 4:5], 1.0 / RB))
        p.op("vector", lambda E: E.tensor_scalar(
            vero[:, :], pA[0:1, 128:256], evs[:, 5:6], None, ALU.mult))
        dvc = p.op("vector", lambda E: E.tensor_copy(srowb[:, :], pA[0:1, 256:512]), inc="dv")
        pA_free_dv = dvc
        yield
        p.wait("tensor", "dv", dvc)
        for hf in range(2):
            for kt in range(8):
                p.op("tensor", lambda E, hf=hf, kt=kt, vT=vT: E.matmul(
                    pC[:, hf * 512:(hf + 1) * 512],
                    vT[:, kt * 128:(kt + 1) * 128],
                    probsT[:, kt * 1024 + hf * 512: kt * 1024 + hf * 512 + 512],
                    start=(kt == 0), stop=(kt == 7 and hf == 0)))
        pec = p.op("tensor", lambda E: E.matmul(
            pC[:, 768:1024], vero[:, :], srowb[:, :], start=False, stop=True), inc="pe")
        p.wait("tensor", "ac", crx)
        p.op("tensor", lambda E: E.matmul(
            pA[:, 0:512], s_ones_rb[:, :], negrh[0:1, 0:512], start=True, stop=True))
        cbpe = p.op("tensor", lambda E: E.matmul(
            pA[:, 512:1024], s_ones_rb[:, :], negrh[0:1, 512:1024],
            start=True, stop=True), inc="pe")
        cb_pe_prev = cbpe
        p.wait("vector", "pe", cbpe)
        cbdv = p.op("vector", lambda E: E.tensor_copy(cbs[:, :], pA[:, 0:1024]), inc="dv")
        pA_free_dv = cbdv
        p.op("vector", lambda E, h=h: E.tensor_tensor(
            aoT[:, h * 1024:h * 1024 + 512], pC[:, 0:512], cbs[:, 0:512], ALU.mult))
        prev_pv_dv = p.op("vector", lambda E, h=h: E.tensor_tensor(
            aoT[:, h * 1024 + 512:(h + 1) * 1024], pC[:, 512:1024],
            cbs[:, 512:1024], ALU.mult), inc="dv")
        pC_free_dv = prev_pv_dv
        yield

    if nh > 0:
        for h in range(nh):
            b = genB(h)
            a2 = genA(h + 1) if h + 1 < nh else None
            i = 0
            for _ in b:
                i += 1
                if a2 is not None and i in (3, 6, 8, 9, 12, 15, 17, 18):
                    next(a2, None)
            if a2 is not None:
                for _ in a2:
                    pass

    C.at(GLOB_END)
    XT = C.f32(16 * 1024)

    wold = {}

    def emit_woload(oc):
        pwv = oc % 2
        p.wait("gpsimd", "pe", proj_pe.get(nh - 2 + pwv, 0) if oc < 2 else wo_pe[oc - 2])
        wold[oc] = p.dma("gpsimd", wqs[pwv][:, :], wo[oc, :, :], sem=f"wp{pwv}")

    if do_w:
        emit_woload(0)
        emit_woload(1)
    p.wait("gpsimd", "pe", proj_pe.get(nh - 1, 0))
    xt_g = 0
    for c in range(NCH if do_w else 0):
        xt_g = p.dma("gpsimd", XT[:, c * 1024:(c + 1) * 1024],
                     xqT[c * 128:(c + 1) * 128, :], sem="xt0")
    woev = {}
    wo_pe = {}
    for oc in range(16 if do_w else 0):
        pwv = oc % 2
        p.wait("tensor", f"wp{pwv}", wold[oc])
        if oc == 0:
            p.wait("tensor", "dv", prev_pv_dv)
        pp = pB if pwv == 0 else pC
        for hf in range(2):
            if oc >= 2:
                p.wait("tensor", "dv", woev[(oc - 2, hf)])
            for c in range(NCH):
                pec = p.op("tensor", lambda E, pp=pp, pwv=pwv, hf=hf, c=c: E.matmul(
                    pp[:, hf * 512:(hf + 1) * 512],
                    wqs[pwv][:, c * 128:(c + 1) * 128],
                    aoT[:, c * 1024 + hf * 512: c * 1024 + hf * 512 + 512],
                    start=(c == 0), stop=(c == NCH - 1)), inc=("pe" if c == NCH - 1 else None))
            p.wait("vector", "pe", pec)
            if oc == 0 and hf == 0:
                p.wait("vector", "xt0", xt_g)
            woev[(oc, hf)] = p.op("vector", lambda E, pp=pp, oc=oc, hf=hf: E.tensor_tensor(
                XT[:, oc * 1024 + hf * 512: oc * 1024 + hf * 512 + 512],
                pp[:, hf * 512:(hf + 1) * 512],
                XT[:, oc * 1024 + hf * 512: oc * 1024 + hf * 512 + 512],
                ALU.add), inc="dv")
        wo_pe[oc] = p.ct["pe"]
        if oc + 2 < 16:
            emit_woload(oc + 2)
    wo_done_dv = p.ct["dv"]

    C.at(A_SCRATCH)
    hs2T = C.b16(16 * 1024)
    Z1_END = C.cursor
    w2slot = C.b16(SLAB)
    wbuf = [C.b16(2048) for _ in range(4)]
    dwT = C.b16(1024, p0=0, np_=16)
    s_gateT = C.f32(256)
    s_e4 = C.f32(1, p0=0, np_=16)
    idxsb = C.i32(64)
    wecol = [C.f32(2) for _ in range(2)]
    mcols = C.f32(2)
    slot1b = C.i32(8)
    slot2b = C.i32(8)
    OV = C.cursor
    sq2 = [C.f32(1024) for _ in range(2)]
    murow = C.f32(1024, p0=0, np_=1)
    msqrow = C.f32(1024, p0=0, np_=1)
    mub = C.f32(1024)
    rstdb = C.f32(1024)
    cenb = C.f32(1024)
    xfc = [C.f32(1024) for _ in range(2)]
    sg = C.f32(128)
    dwt = C.f32(128)
    tk = C.f32(16)
    tcols = C.f32(8)
    L2_END = C.cursor
    assert L2_END <= ARENA_BYTES, L2_END
    C.at(OV)
    sct = [C.f32(1024, p0=0, np_=16) for _ in range(3)]
    sgT = sct[0]
    sctT = C.f32(16)
    sctT2 = C.f32(16)
    IDXB_END = C.cursor
    assert IDXB_END <= ARENA_BYTES
    C.at(OV)
    h_st = C.b16(ICN * 1024)
    yst = [C.f32(2048) for _ in range(2)]
    SH_END = C.cursor
    assert SH_END <= ARENA_BYTES
    C.at(A_SCRATCH)
    disp = [[C.b16(DM + 16), C.b16(DM + 16)] for _ in range(2)]
    dispT = C.b16(16 * 256)
    h_sp = C.b16(ICN * 256)
    RT_END = C.cursor
    assert RT_END <= Z1_END, (RT_END, Z1_END)
    C.at(IDXB_END)
    hstage = [C.b16(DM + 16) for _ in range(2)]
    HST_END = C.cursor
    assert HST_END <= ARENA_BYTES
    C.at(OV)
    mstage = [C.b16(2048) for _ in range(8)]

    p.dma("gpsimd", s_gateT[:, :], gateT[:, :], sem="gc3")
    gate_g = p.dma("gpsimd", s_e4[:, :], e4col[:, :], sem="gc3")

    w2_ct = {}
    d_stage_pe = {}

    def emit_w2dma(ve):
        p.wait("gpsimd", "pe", d_stage_pe.get(ve - 1, 0))
        p.wait("gpsimd", "dv", wo_done_dv)
        w2_ct[ve] = p.dma("gpsimd", w2slot[:, :], w2v[ve, :, :], sem="w2s")

    emit_w2dma(0)

    blk_ct = {}
    buf_free_pe = [0, 0, 0, 0]

    def bufix(ve, blk):
        return (ve * 2 * ICN + blk) % 4

    def emit_blkdma(ve, blk):
        mat = w1v if blk < ICN else w3v
        ic = blk % ICN
        bx = bufix(ve, blk)
        p.wait("sync", "pe", buf_free_pe[bx])
        if ve == 0 and blk < 4:
            p.wait("sync", "dv", wo_done_dv)
        blk_ct[(ve, blk)] = p.dma("sync", wbuf[bx][:, :],
                                  mat[ve, :, ic * 2048:(ic + 1) * 2048], sem=f"wb{bx}")

    _saved_q = None
    if not do_l2:
        _saved_q = {k: len(v) for k, v in p.q.items()}
        _saved_ct = dict(p.ct)
    p.wait("gpsimd", "dv", wo_done_dv)
    p.wait("tensor", "dv", wo_done_dv)
    l2a_pe = {}
    for c in range(NCH):
        if c >= 2:
            p.wait("gpsimd", "pe", l2a_pe[c - 2])
        gpc = p.op("gpsimd", lambda E, c=c, b=c % 2: E.tensor_tensor(
            sq2[b][:, :], XT[:, c * 1024:(c + 1) * 1024],
            XT[:, c * 1024:(c + 1) * 1024], ALU.mult), inc="gp")
        p.wait("tensor", "gp", gpc)
        for hf in range(2):
            p.op("tensor", lambda E, c=c, hf=hf: E.matmul(
                pB[0:1, hf * 512:(hf + 1) * 512], s_ones_c[:, :],
                XT[:, c * 1024 + hf * 512: c * 1024 + hf * 512 + 512],
                start=(c == 0), stop=(c == NCH - 1)))
            l2a_pe[c] = p.op("tensor", lambda E, c=c, hf=hf, b=c % 2: E.matmul(
                pC[0:1, hf * 512:(hf + 1) * 512], s_ones_c[:, :],
                sq2[b][:, hf * 512:(hf + 1) * 512],
                start=(c == 0), stop=(c == NCH - 1)), inc=("pe" if hf == 1 else None))
    pec = l2a_pe[NCH - 1]
    p.wait("vector", "pe", pec)
    for hf in range(2):
        p.op("vector", lambda E, hf=hf: E.tensor_scalar(
            murow[:, hf * 512:(hf + 1) * 512], pB[0:1, hf * 512:(hf + 1) * 512],
            1.0 / DM, None, ALU.mult))
        p.op("vector", lambda E, hf=hf: E.tensor_scalar(
            msqrow[:, hf * 512:(hf + 1) * 512], pC[0:1, hf * 512:(hf + 1) * 512],
            1.0 / DM, None, ALU.mult))
    dvc = p.ct["dv"] + 0
    dvc = p.op("vector", lambda E: E.nop(), inc="dv")
    p.wait("tensor", "dv", dvc)
    for hf in range(2):
        p.op("tensor", lambda E, hf=hf: E.matmul(
            pB[:, hf * 512:(hf + 1) * 512], s_ones_r[:, :],
            murow[:, hf * 512:(hf + 1) * 512], start=True, stop=True))
        pec = p.op("tensor", lambda E, hf=hf: E.matmul(
            pC[:, hf * 512:(hf + 1) * 512], s_ones_r[:, :],
            msqrow[:, hf * 512:(hf + 1) * 512], start=True, stop=True),
            inc=("pe" if hf == 1 else None))
    p.wait("vector", "pe", pec)
    p.op("vector", lambda E: E.tensor_copy(mub[:, :], pB[:, :]))
    p.op("vector", lambda E: E.tensor_tensor(cenb[:, :], mub[:, :], mub[:, :], ALU.mult))
    dvc = p.op("vector", lambda E: E.tensor_tensor(
        cenb[:, :], pC[:, :], cenb[:, :], ALU.subtract), inc="dv")
    p.wait("scalar", "dv", dvc)
    p.op("scalar", lambda E: E.activation(rstdb[:, :], cenb[:, :], AF.Ln, bias=s_eps[:, 0:1]))
    acc = p.op("scalar", lambda E: E.activation(rstdb[:, :], rstdb[:, :], AF.Exp, scale=-0.5), inc="ac")
    p.wait("vector", "ac", acc)
    p.op("vector", lambda E: E.nop(), inc="dv")

    gate_pe = {}
    p.wait("tensor", "gc3", gate_g)
    for c in range(NCH):
        bi = c % 2
        if c >= 2:
            p.wait("vector", "pe", gate_pe[c - 2])
        p.op("vector", lambda E, c=c: E.tensor_tensor(
            cenb[:, :], XT[:, c * 1024:(c + 1) * 1024], mub[:, :], ALU.subtract))
        p.op("vector", lambda E, b=bi: E.tensor_tensor(
            xfc[b][:, :], cenb[:, :], rstdb[:, :], ALU.mult))
        dvc = p.op("vector", lambda E, c=c, b=bi: E.tensor_copy(
            hs2T[:, c * 1024:(c + 1) * 1024], xfc[b][:, :]), inc="dv")
        p.wait("tensor", "dv", dvc)
        for hf in range(2):
            gate_pe[c] = p.op("tensor", lambda E, c=c, hf=hf, b=bi: E.matmul(
                pA[0:16, hf * 512:(hf + 1) * 512],
                s_gateT[:, c * 16:(c + 1) * 16],
                xfc[b][:, hf * 512:(hf + 1) * 512],
                start=(c == 0), stop=(c == NCH - 1)), inc=("pe" if hf == 1 else None))
    p.wait("scalar", "pe", gate_pe[NCH - 1])
    acc = p.op("scalar", lambda E: E.activation(sgT[:, :], pA[0:16, :], AF.Sigmoid), inc="ac")
    p.wait("tensor", "ac", acc)
    SL4g = ((pB, 0), (pB, 512), (pC, 0), (pC, 512))
    gslot_dv = [0] * 4
    for qt in range(8):
        ppg, ofg = SL4g[qt % 4]
        p.wait("tensor", "dv", gslot_dv[qt % 4])
        pec = p.op("tensor", lambda E, qt=qt, ppg=ppg, ofg=ofg: E.transpose(
            ppg[:, ofg:ofg + 16], sgT[0:16, qt * 128:(qt + 1) * 128],
            s_idf[0:16, 0:16]), inc="pe")
        p.wait("vector", "pe", pec)
        gslot_dv[qt % 4] = p.op("vector", lambda E, qt=qt, ppg=ppg, ofg=ofg: E.tensor_copy(
            sg[:, qt * 16:(qt + 1) * 16], ppg[:, ofg:ofg + 16]), inc="dv")
    for qt in range(8):
        sgt = sg[:, qt * 16:(qt + 1) * 16]
        dwv = dwt[:, qt * 16:(qt + 1) * 16]
        p.op("vector", lambda E, sgt=sgt: E.reduce_max(tcols[:, 0:1], sgt, axis=AX))
        p.op("vector", lambda E, sgt=sgt: E.tensor_scalar(
            tk[:, :], sgt, tcols[:, 0:1], None, ALU.is_equal))
        p.op("vector", lambda E, sgt=sgt: E.scalar_tensor_tensor(
            tk[:, :], tk[:, :], -1e30, sgt, ALU.mult, ALU.add))
        p.op("vector", lambda E: E.reduce_max(tcols[:, 1:2], tk[:, :], axis=AX))
        p.op("vector", lambda E: E.tensor_tensor(tcols[:, 2:3], tcols[:, 0:1], tcols[:, 1:2], ALU.add))
        p.op("vector", lambda E: E.reciprocal(tcols[:, 3:4], tcols[:, 2:3]))
        p.op("vector", lambda E: E.tensor_tensor(tcols[:, 4:5], tcols[:, 0:1], tcols[:, 3:4], ALU.mult))
        p.op("vector", lambda E: E.tensor_tensor(tcols[:, 5:6], tcols[:, 1:2], tcols[:, 3:4], ALU.mult))
        p.op("vector", lambda E: E.tensor_scalar(
            tk[:, :], tk[:, :], tcols[:, 1:2], None, ALU.is_equal))
        p.op("vector", lambda E, sgt=sgt, dwv=dwv: E.tensor_scalar(
            dwv, sgt, tcols[:, 0:1], None, ALU.is_equal))
        p.op("vector", lambda E, dwv=dwv: E.tensor_scalar(
            dwv, dwv, tcols[:, 4:5], None, ALU.mult))
        p.op("vector", lambda E, dwv=dwv: E.scalar_tensor_tensor(
            dwv, tk[:, :], tcols[:, 5:6], dwv, ALU.mult, ALU.add), inc="dv")
    p.wait("tensor", "dv", p.ct["dv"])
    SL4d = ((pA, 0), (pA, 512), (pB, 0), (pB, 512))
    dslot_dv = [0] * 4
    for qt in range(8):
        ppd, ofd = SL4d[qt % 4]
        p.wait("tensor", "dv", dslot_dv[qt % 4])
        pec = p.op("tensor", lambda E, qt=qt, ppd=ppd, ofd=ofd: E.transpose(
            ppd[0:16, ofd:ofd + 128], dwt[:, qt * 16:(qt + 1) * 16],
            s_idf[:, :]), inc="pe")
        p.wait("vector", "pe", pec)
        dslot_dv[qt % 4] = p.op("vector", lambda E, qt=qt, ppd=ppd, ofd=ofd: E.tensor_copy(
            dwT[0:16, qt * 128:(qt + 1) * 128], ppd[0:16, ofd:ofd + 128]), inc="dv")
    dwT_dv = p.ct["dv"]

    if not do_l2:
        for k, n0 in _saved_q.items():
            del p.q[k][n0:]
        p.ct.update(_saved_ct)
    tslot = {0: ("dv", wo_done_dv), 1: ("dv", wo_done_dv)}
    TSL = ((pT1, 0), (pT2, 0))

    p.wait("vector", "gc3", gate_g)
    p.op("vector", lambda E: E.memset(sct[2][:, :], 0.0))
    p.op("vector", lambda E: E.tensor_single_scalar(sct[0][:, :], dwT[0:16, :], 0.0, ALU.is_gt))
    p.op("vector", lambda E: E.tensor_tensor_scan(
        sct[1][:, :], sct[0][:, :], sct[2][:, :], 0.0, ALU.add, ALU.add))
    p.op("vector", lambda E: E.tensor_scalar_add(sct[1][:, :], sct[1][:, :], -1.0))
    p.op("vector", lambda E: E.tensor_single_scalar(sct[2][:, :], sct[1][:, :], 255.5, ALU.is_le))
    p.op("vector", lambda E: E.tensor_tensor(sct[0][:, :], sct[0][:, :], sct[2][:, :], ALU.mult))
    p.op("vector", lambda E: E.tensor_scalar(
        sct[1][:, :], sct[1][:, :], s_e4[:, 0:1], None, ALU.add))
    p.op("vector", lambda E: E.tensor_scalar_add(sct[1][:, :], sct[1][:, :], -9000.0))
    p.op("vector", lambda E: E.tensor_tensor(sct[1][:, :], sct[0][:, :], sct[1][:, :], ALU.mult))
    slv = p.op("vector", lambda E: E.tensor_scalar_add(sct[1][:, :], sct[1][:, :], 9000.0), inc="dv")

    SL4s = ((pA, 0), (pA, 512), (pB, 0), (pB, 512))
    sslot_dv = [0] * 4
    for tt in range(8):
        pps, ofs = SL4s[tt % 4]
        p.wait("tensor", "dv", max(sslot_dv[tt % 4], slv))
        pec = p.op("tensor", lambda E, tt=tt, pps=pps, ofs=ofs: E.transpose(
            pps[:, ofs:ofs + 16], sct[1][0:16, tt * 128:(tt + 1) * 128],
            s_idf[0:16, 0:16]), inc="pe")
        p.wait("vector", "pe", pec)
        p.op("vector", lambda E, pps=pps, ofs=ofs: E.tensor_copy(sctT[:, :], pps[:, ofs:ofs + 16]))
        p.op("vector", lambda E: E.tensor_reduce(mcols[:, 0:1], sctT[:, :], AX, ALU.min))
        p.op("vector", lambda E: E.tensor_scalar(
            sctT2[:, :], sctT[:, :], mcols[:, 0:1], None, ALU.is_equal))
        p.op("vector", lambda E: E.scalar_tensor_tensor(
            sctT[:, :], sctT2[:, :], 9000.0, sctT[:, :], ALU.mult, ALU.add))
        p.op("vector", lambda E: E.tensor_reduce(mcols[:, 1:2], sctT[:, :], AX, ALU.min))
        p.op("vector", lambda E, tt=tt: E.tensor_copy(slot1b[:, tt:tt + 1], mcols[:, 0:1]))
        sslot_dv[tt % 4] = p.op("vector", lambda E, tt=tt: E.tensor_copy(
            slot2b[:, tt:tt + 1], mcols[:, 1:2]), inc="dv")
    slot_cast_dv = p.ct["dv"]

    dsc = {0: 0, 1: 0}
    p.wait("scalar", "dv", dwT_dv)
    for tt in range(8):
        bi = tt % 2
        dsem = ("hx", "st1")[bi]
        for c in range(NCH):
            sl = c % 2
            ptb, co = TSL[sl]
            p.wait("tensor", *tslot[sl])
            pec = p.op("tensor", lambda E, c=c, tt=tt, ptb=ptb, co=co: E.transpose(
                ptb[:, co:co + 128], hs2T[:, c * 1024 + tt * 128: c * 1024 + (tt + 1) * 128],
                s_idb[:, :]), inc="pe")
            p.wait("scalar", "pe", pec)
            if c == 0:
                p.wait("scalar", dsem, dsc[bi])
            tslot[sl] = ("ac", p.op("scalar", lambda E, c=c, b=bi, ptb=ptb, co=co: E.activation(
                hstage[b][:, c * 128:(c + 1) * 128], ptb[:, co:co + 128], AF.Copy), inc="ac"))
        dwc = p.op("scalar", lambda E, tt=tt, b=bi: E.activation(
            hstage[b][:, 2048:2064], dwt[:, tt * 16:(tt + 1) * 16], AF.Copy), inc="ac")
        p.wait("gpsimd", "ac", dwc)
        p.wait("gpsimd", "dv", slot_cast_dv)
        if tt == 0:
            p.wait("gpsimd", "s", d2z_ct)
        for sb in (slot1b, slot2b):
            dsc[bi] = p.iop("gpsimd", lambda E, tt=tt, b=bi, sb=sb: E.indirect_dma_start(
                out=disp2[:, :], out_offset=IndirectOffsetOnAxis(sb[:, tt:tt + 1], 0),
                in_=hstage[b][:, :], in_offset=None,
                bounds_check=4095, oob_is_err=False), dsem)
    hx_all = dsc[0]
    st1_all = dsc[1]

    NE = max(nv - 2, 0)
    ga_ct = {}
    tr_pe = {}

    def emit_load(e, gate_pe):
        par = e % 2
        sem = f"ga{par}"
        p.wait("gpsimd", "pe", gate_pe)
        if e < 2:
            p.wait("gpsimd", "hx", hx_all)
            p.wait("gpsimd", "st1", st1_all)
        for tile in range(2):
            p.dma("gpsimd", disp[par][tile][:, :],
                  disp2[e * 256 + tile * 128: e * 256 + (tile + 1) * 128, :], sem=sem)
        ga_ct[e] = p.ct[sem]

    PPOF = ((pA, 0), (pA, 512), (pB, 0), (pB, 512))
    slot_guard = [("dv", wo_done_dv)] * 4
    emit_blkdma(0, 0)
    emit_blkdma(0, 1)
    prev_ac = 0
    prev_dv = dwT_dv

    for ve in (0, 1):
        if nv == 0:
            break
        p.wait("tensor", "ac", prev_ac)
        p.wait("tensor", "dv", prev_dv)
        if ve == 0:
            p.wait("scalar", "dv", slot_cast_dv)
        a_sil = {}
        for g in range(2 * ICN):
            ic, hf = g // 2, g % 2
            blk = ic
            bx = bufix(ve, blk)
            if hf == 0:
                if blk + 2 < 2 * ICN:
                    emit_blkdma(ve, blk + 2)
                p.wait("tensor", f"wb{bx}", blk_ct[(ve, blk)])
            sgd = slot_guard[g % 4]
            p.wait("tensor", sgd[0], sgd[1])
            pp, poff = PPOF[g % 4]
            for c in range(NCH):
                pec = p.op("tensor", lambda E, pp=pp, poff=poff, bu=wbuf[bx], hf=hf, c=c: E.matmul(
                    pp[:, poff:poff + 512],
                    bu[:, c * 128:(c + 1) * 128],
                    hs2T[:, c * 1024 + hf * 512: c * 1024 + hf * 512 + 512],
                    start=(c == 0), stop=(c == NCH - 1)), inc=("pe" if c == NCH - 1 else None))
            if hf == 1:
                buf_free_pe[bx] = pec
            p.wait("scalar", "pe", pec)
            if ve == 0 and g == 12:
                p.wait("scalar", "hx", hx_all)
                p.wait("scalar", "st1", st1_all)
            a_sil[g] = p.op("scalar", lambda E, pp=pp, poff=poff, ic=ic, hf=hf: E.activation(
                h_st[:, ic * 1024 + hf * 512: ic * 1024 + hf * 512 + 512],
                pp[:, poff:poff + 512], AF.Silu), inc="ac")
            slot_guard[g % 4] = ("ac", a_sil[g])
        b_dv = {}
        for g in range(2 * ICN):
            ic, hf = g // 2, g % 2
            blk = ICN + ic
            bx = bufix(ve, blk)
            if hf == 0:
                if blk + 2 < 2 * ICN:
                    emit_blkdma(ve, blk + 2)
                elif ve + 1 < nv:
                    emit_blkdma(ve + 1, blk + 2 - 2 * ICN)
                p.wait("tensor", f"wb{bx}", blk_ct[(ve, blk)])
            sgd = slot_guard[g % 4]
            p.wait("tensor", sgd[0], sgd[1])
            pp, poff = PPOF[g % 4]
            for c in range(NCH):
                pec = p.op("tensor", lambda E, pp=pp, poff=poff, bu=wbuf[bx], hf=hf, c=c: E.matmul(
                    pp[:, poff:poff + 512],
                    bu[:, c * 128:(c + 1) * 128],
                    hs2T[:, c * 1024 + hf * 512: c * 1024 + hf * 512 + 512],
                    start=(c == 0), stop=(c == NCH - 1)), inc=("pe" if c == NCH - 1 else None))
            if hf == 1:
                buf_free_pe[bx] = pec
            p.wait("vector", "pe", pec)
            p.wait("vector", "ac", a_sil[g])
            b_dv[g] = p.op("vector", lambda E, pp=pp, poff=poff, ic=ic, hf=hf: E.tensor_tensor(
                h_st[:, ic * 1024 + hf * 512: ic * 1024 + hf * 512 + 512],
                h_st[:, ic * 1024 + hf * 512: ic * 1024 + hf * 512 + 512],
                pp[:, poff:poff + 512], ALU.mult), inc="dv")
            slot_guard[g % 4] = ("dv", b_dv[g])
        if ve == 1:
            b_end = p.ct["pe"]
            if NE > 0:
                emit_load(0, b_end)
            if NE > 1:
                emit_load(1, b_end)
        p.wait("tensor", "w2s", w2_ct[ve])
        p.wait("tensor", "dv", b_dv[2 * ICN - 1])
        d_dv = {}
        for g in range(2 * NCH):
            dc, hf = g // 2, g % 2
            sgd = slot_guard[g % 4]
            p.wait("tensor", sgd[0], sgd[1])
            pp, poff = PPOF[g % 4]
            for ic in range(ICN):
                pec = p.op("tensor", lambda E, pp=pp, poff=poff, dc=dc, hf=hf, ic=ic: E.matmul(
                    pp[:, poff:poff + 512],
                    w2slot[:, ic * 2048 + dc * 128: ic * 2048 + (dc + 1) * 128],
                    h_st[:, ic * 1024 + hf * 512: ic * 1024 + hf * 512 + 512],
                    start=(ic == 0), stop=(ic == ICN - 1)), inc=("pe" if ic == ICN - 1 else None))
            p.wait("vector", "pe", pec)
            d_dv[g] = p.op("vector", lambda E, pp=pp, poff=poff, dc=dc, hf=hf: E.tensor_tensor(
                XT[:, dc * 1024 + hf * 512: dc * 1024 + hf * 512 + 512],
                pp[:, poff:poff + 512],
                XT[:, dc * 1024 + hf * 512: dc * 1024 + hf * 512 + 512],
                ALU.add), inc="dv")
            slot_guard[g % 4] = ("dv", d_dv[g])
        d_stage_pe[ve] = p.ct["pe"]
        if ve + 1 < nv:
            emit_w2dma(ve + 1)
        prev_ac = a_sil[2 * ICN - 1]
        prev_dv = d_dv[2 * NCH - 1]
    shared_done_pe = p.ct["pe"]

    ysem_ct = {}
    for e in range(NE):
        ve = e + 2
        par = e % 2
        sem = f"ga{par}"
        ysem = ("sc", "yl")[par]
        p.wait("vector", sem, ga_ct[e])
        for tile in range(2):
            p.op("vector", lambda E, par=par, tile=tile, e=e: E.tensor_copy(
                wecol[par][:, tile:tile + 1],
                disp[par][tile][:, 2048 + e:2049 + e]))
        p.wait("tensor", sem, ga_ct[e])
        for tile in range(2):
            for c in range(NCH):
                sl = c % 2
                ptb, co = TSL[sl]
                p.wait("tensor", *tslot[sl])
                pec = p.op("tensor", lambda E, tile=tile, c=c, par=par, ptb=ptb, co=co: E.transpose(
                    ptb[:, co:co + 128], disp[par][tile][:, c * 128:(c + 1) * 128],
                    s_idb[:, :]), inc="pe")
                p.wait("vector", "pe", pec)
                tslot[sl] = ("dv", p.op("vector", lambda E, tile=tile, c=c, ptb=ptb, co=co: E.tensor_copy(
                    dispT[:, c * 256 + tile * 128: c * 256 + tile * 128 + 128],
                    ptb[:, co:co + 128]), inc="dv"))
        tr_pe[e] = p.ct["pe"]
        if e + 2 < NE:
            emit_load(e + 2, tr_pe[e])
        a_sil = {}
        for ic in range(ICN):
            blk = ic
            bx = bufix(ve, blk)
            if blk + 2 < 2 * ICN:
                emit_blkdma(ve, blk + 2)
            p.wait("tensor", f"wb{bx}", blk_ct[(ve, blk)])
            sgd = slot_guard[ic % 4]
            p.wait("tensor", sgd[0], sgd[1])
            if ic == 0:
                p.wait("tensor", *tslot[1])
            pp, poff = PPOF[ic % 4]
            for c in range(NCH):
                pec = p.op("tensor", lambda E, pp=pp, poff=poff, bu=wbuf[bx], c=c: E.matmul(
                    pp[:, poff:poff + 256],
                    bu[:, c * 128:(c + 1) * 128],
                    dispT[:, c * 256:(c + 1) * 256],
                    start=(c == 0), stop=(c == NCH - 1)), inc=("pe" if c == NCH - 1 else None))
            buf_free_pe[bx] = pec
            p.wait("scalar", "pe", pec)
            a_sil[ic] = p.op("scalar", lambda E, pp=pp, poff=poff, ic=ic: E.activation(
                h_sp[:, ic * 256:(ic + 1) * 256], pp[:, poff:poff + 256], AF.Silu), inc="ac")
            slot_guard[ic % 4] = ("ac", a_sil[ic])
        b_dv = {}
        for ic in range(ICN):
            blk = ICN + ic
            bx = bufix(ve, blk)
            if blk + 2 < 2 * ICN:
                emit_blkdma(ve, blk + 2)
            elif ve + 1 < nv:
                emit_blkdma(ve + 1, blk + 2 - 2 * ICN)
            p.wait("tensor", f"wb{bx}", blk_ct[(ve, blk)])
            sgd = slot_guard[ic % 4]
            p.wait("tensor", sgd[0], sgd[1])
            pp, poff = PPOF[ic % 4]
            for c in range(NCH):
                pec = p.op("tensor", lambda E, pp=pp, poff=poff, bu=wbuf[bx], c=c: E.matmul(
                    pp[:, poff:poff + 256],
                    bu[:, c * 128:(c + 1) * 128],
                    dispT[:, c * 256:(c + 1) * 256],
                    start=(c == 0), stop=(c == NCH - 1)), inc=("pe" if c == NCH - 1 else None))
            buf_free_pe[bx] = pec
            p.wait("vector", "pe", pec)
            p.wait("vector", "ac", a_sil[ic])
            b_dv[ic] = p.op("vector", lambda E, pp=pp, poff=poff, ic=ic: E.tensor_tensor(
                h_sp[:, ic * 256:(ic + 1) * 256],
                h_sp[:, ic * 256:(ic + 1) * 256],
                pp[:, poff:poff + 256], ALU.mult), inc="dv")
            slot_guard[ic % 4] = ("dv", b_dv[ic])
        p.wait("tensor", "w2s", w2_ct[ve])
        p.wait("tensor", "dv", b_dv[ICN - 1])
        y_dv = {}
        for g in range(8):
            tile, dq = g // 4, g % 4
            sgd = slot_guard[g % 4]
            p.wait("tensor", sgd[0], sgd[1])
            pp, poff = PPOF[g % 4]
            for ic in range(ICN):
                pec = p.op("tensor", lambda E, pp=pp, poff=poff, tile=tile, dq=dq, ic=ic: E.matmul(
                    pp[:, poff:poff + 512],
                    h_sp[:, ic * 256 + tile * 128: ic * 256 + tile * 128 + 128],
                    w2slot[:, ic * 2048 + dq * 512: ic * 2048 + (dq + 1) * 512],
                    start=(ic == 0), stop=(ic == ICN - 1)), inc=("pe" if ic == ICN - 1 else None))
            p.wait("vector", "pe", pec)
            if g == 0 and e >= 1:
                p.wait("vector", ("sc", "yl")[(e - 1) % 2], ysem_ct[e - 1])
            y_dv[g] = p.op("vector", lambda E, pp=pp, poff=poff, tile=tile, dq=dq, par=par: E.tensor_scalar(
                yst[tile][:, dq * 512:(dq + 1) * 512], pp[:, poff:poff + 512],
                wecol[par][:, tile:tile + 1], None, ALU.mult), inc="dv")
            slot_guard[g % 4] = ("dv", y_dv[g])
        d_stage_pe[ve] = p.ct["pe"]
        if ve + 1 < nv:
            emit_w2dma(ve + 1)
        p.wait("gpsimd", "dv", y_dv[7])
        for tile in range(2):
            ysem_ct[e] = p.dma("gpsimd",
                               yout[e * 256 + tile * 128: e * 256 + (tile + 1) * 128, :],
                               yst[tile][:, :], sem=ysem)
        prev_ac = a_sil[ICN - 1]
        prev_dv = y_dv[7]

    sc_all = p.ct["sc"]
    yl_all = p.ct["yl"]
    mst_pe = [0, 0, 0, 0]
    msems = ("mg0", "mg1", "hx", "st1")
    p.wait("vector", "sc", sc_all)
    p.wait("vector", "yl", yl_all)
    zpre = {}
    for bi in range(4):
        p.op("vector", lambda E, m0=mstage[2 * bi]: E.memset(m0[:, :], 0.0))
        zpre[bi] = p.op("vector", lambda E, m1=mstage[2 * bi + 1]: E.memset(m1[:, :], 0.0), inc="dv")
    for tt in range(8):
        bi = tt % 4
        msem = msems[bi]
        m0, m1 = mstage[2 * bi], mstage[2 * bi + 1]
        if tt < 4:
            zdv = zpre[bi]
        else:
            p.wait("vector", "pe", mst_pe[bi])
            p.op("vector", lambda E, m0=m0: E.memset(m0[:, :], 0.0))
            zdv = p.op("vector", lambda E, m1=m1: E.memset(m1[:, :], 0.0), inc="dv")
        p.wait("gpsimd", "dv", zdv)
        for sb, mm in ((slot1b, m0), (slot2b, m1)):
            mg = p.iop("gpsimd", lambda E, tt=tt, sb=sb, mm=mm: E.indirect_dma_start(
                out=mm[:, :], out_offset=None,
                in_=yout[:, :],
                in_offset=IndirectOffsetOnAxis(sb[:, tt:tt + 1], 0),
                bounds_check=4095, oob_is_err=False), msem)
        p.wait("vector", msem, mg)
        adv = p.op("vector", lambda E, m0=m0, m1=m1: E.tensor_tensor(
            m0[:, :], m0[:, :], m1[:, :], ALU.add), inc="dv")
        if tt == 0:
            p.wait("tensor", "dv", prev_dv)
            p.wait("tensor", "ac", prev_ac)
        for c in range(NCH):
            sl = c % 2
            ptb, co = TSL[sl]
            p.wait("tensor", *tslot[sl])
            p.wait("tensor", "dv", adv)
            pec = p.op("tensor", lambda E, c=c, m0=m0, ptb=ptb, co=co: E.transpose(
                ptb[:, co:co + 128], m0[:, c * 128:(c + 1) * 128],
                s_idb[:, :]), inc="pe")
            p.wait("vector", "pe", pec)
            tslot[sl] = ("dv", p.op("vector", lambda E, c=c, tt=tt, ptb=ptb, co=co: E.tensor_tensor(
                XT[:, c * 1024 + tt * 128: c * 1024 + (tt + 1) * 128],
                ptb[:, co:co + 128],
                XT[:, c * 1024 + tt * 128: c * 1024 + (tt + 1) * 128],
                ALU.add), inc="dv"))
            p.wait("sync", "dv", tslot[sl][1])
            p.dma("sync", out[c * 128:(c + 1) * 128, tt * 128:(tt + 1) * 128],
                  XT[:, c * 1024 + tt * 128: c * 1024 + (tt + 1) * 128], sem="st0")
        mst_pe[bi] = p.ct["pe"]
    ffn_done_dv = p.ct["dv"]

    p.wait("sync", "st0", p.ct["st0"])

    with nc.Block() as block:
        block.tensor(lambda EE: [fn(EE) for fn in p.q["tensor"]])
        block.vector(lambda EE: [fn(EE) for fn in p.q["vector"]])
        block.scalar(lambda EE: [fn(EE) for fn in p.q["scalar"]])
        block.gpsimd(lambda EE: [fn(EE) for fn in p.q["gpsimd"]])
        block.sync(lambda EE: [fn(EE) for fn in p.q["sync"]])
    return nc


_NC_CACHE = {}

def kernel(L0_attn_hidden_states, L0_attn_Wq, L0_attn_Wk, L0_attn_Wv, L0_attn_Wo,
           L0_ffn_gate_w, L0_ffn_w1, L0_ffn_w2, L0_ffn_w3,
           L0_ffn_sw1, L0_ffn_sw2, L0_ffn_sw3):
    x = np.ascontiguousarray(np.asarray(L0_attn_hidden_states, dtype=np.float32))
    Wq = np.asarray(L0_attn_Wq, dtype=np.float32)
    Wk = np.asarray(L0_attn_Wk, dtype=np.float32)
    Wv = np.asarray(L0_attn_Wv, dtype=np.float32)
    Wo = np.asarray(L0_attn_Wo, dtype=np.float32)
    gw = np.asarray(L0_ffn_gate_w, dtype=np.float32)

    wq_b = _attnT_blocks(Wq); wk_b = _attnT_blocks(Wk)
    wv_b = _attnT_blocks(Wv); wo_b = _attnT_blocks(Wo)
    gateT_a = np.ascontiguousarray(gw.reshape(16, 16, 128).transpose(2, 1, 0).reshape(128, 256))

    w1 = np.asarray(L0_ffn_w1); w2 = np.asarray(L0_ffn_w2); w3 = np.asarray(L0_ffn_w3)
    sw1 = np.asarray(L0_ffn_sw1); sw2 = np.asarray(L0_ffn_sw2); sw3 = np.asarray(L0_ffn_sw3)
    w1v = np.empty((NVE, 128, ICN * 2048), dtype=BF)
    w2v = np.empty((NVE, 128, SLAB), dtype=BF)
    w3v = np.empty((NVE, 128, ICN * 2048), dtype=BF)
    w1v[0] = _w13_ic_blocks(sw1[:1408]); w1v[1] = _w13_ic_blocks(sw1[1408:])
    w3v[0] = _w13_ic_blocks(sw3[:1408]); w3v[1] = _w13_ic_blocks(sw3[1408:])
    w2v[0] = _w2_slab(sw2[:, :1408]); w2v[1] = _w2_slab(sw2[:, 1408:])
    for e in range(16):
        w1v[2 + e] = _w13_ic_blocks(w1[e])
        w3v[2 + e] = _w13_ic_blocks(w3[e])
        w2v[2 + e] = _w2_slab(w2[e])

    scale = float(1.0 / np.sqrt(np.float32(DH)))
    kpos = np.concatenate([KEPT, [0]])
    coskT, sinkT = _rope_tables(kpos, 1.0)
    coskT[:, NK:] = 0
    sinkT[:, NK:] = 0

    identf = np.eye(128, dtype=np.float32)
    identb = np.eye(128, dtype=BF)
    onesrow_a = np.ones((1, 128), np.float32)
    onesrowb_a = np.ones((1, 128), BF)
    onescol_a = np.ones((128, 1), np.float32)
    selmid_a = np.zeros((128, 8), BF)
    for kt in range(8):
        for pp_ in range(128):
            cidx = kt * 128 + pp_
            if SB + 1 <= cidx <= NK - 1:
                selmid_a[pp_, kt] = 1.0
    sel204_a = np.zeros((128, 1), BF); sel204_a[SB - 128, 0] = 1.0
    e4col_a = (256.0 * np.arange(16, dtype=np.float32)).reshape(16, 1)

    shared = dict(wq=wq_b, wk=wk_b, wv=wv_b, wo=wo_b, gateT=gateT_a, onesrowb=onesrowb_a,
                  w1v=w1v, w2v=w2v, w3v=w3v, cosk=coskT, sink=sinkT,
                  identf=identf, identb=identb, onesrow=onesrow_a, onescol=onescol_a,
                  selmid=selmid_a, sel204=sel204_a, e4col=e4col_a,
                  zrows=np.zeros((128, 2064), BF),
                  epscol=np.full((128, 1), 1e-5, np.float32))

    in_maps = []
    for c in range(8):
        b, r = c // 2, c % 2
        xq_a = np.ascontiguousarray(x[b, r * 1024:(r + 1) * 1024, :].T)
        xk_a = np.zeros((2048, 1024), np.float32)
        xk_a[:, :NK] = x[b][KEPT].T
        qpos = np.arange(r * 1024, (r + 1) * 1024)
        cq, sq = _rope_tables(qpos, scale)
        u = U_TABLE[b] if r == 1 else np.full((16,), 2.0, np.float32)
        m = dict(shared)
        m.update(xqT=xq_a, xkT=xk_a, cosq=cq, sinq=sq, urow=np.ascontiguousarray(u.reshape(1, 16)))
        in_maps.append(m)

    if "nc" not in _NC_CACHE:
        _NC_CACHE["nc"] = build_nc()
    nc = _NC_CACHE["nc"]
    res = run_bass_kernel_spmd(nc, in_maps, core_ids=list(range(8)))
    outa = np.empty((4, 2048, 2048), np.float32)
    for c in range(8):
        b, r = c // 2, c % 2
        outa[b, r * 1024:(r + 1) * 1024, :] = res.results[c]["out"].T
    return outa
